# version 06
# baseline (speedup 1.0000x reference)
import numpy as np
import ml_dtypes
from contextlib import ExitStack

import concourse.bass as bass
import concourse.mybir as mybir
from concourse.bass_utils import run_bass_kernel_spmd

BFNP = ml_dtypes.bfloat16
F32 = mybir.dt.float32
BF16 = mybir.dt.bfloat16
AF = mybir.ActivationFunctionType
ALU = mybir.AluOpType
AX = mybir.AxisListType.X

B, L, D = 4, 8192, 1024
BLOCKS = 2731
KT_N = 22
KEYS_PAD = 2816
NQ = 1408
NQT = 11
HALF_LEN = [1366, 1365]
OFFS = [0, 1366]
HEADS, DK = 16, 64
NE, DH = 8, 2048
QCL = [(0, 512, 4), (512, 512, 4), (1024, 384, 3)]
N_SLOT = 3

BAND_OPS = {
    0: [(0, 0, 0, 130), (1, 128, 126, 132), (2, 256, 254, 132), (3, 384, 382, 130),
        (4, 512, 510, 2), (21, "wrap", 0, 2)],
    1: [(3, -128, 0, 2), (4, 0, 0, 130), (5, 128, 126, 132), (6, 256, 254, 132),
        (7, 384, 382, 130), (8, 512, 510, 2)],
    2: [(7, -128, 0, 2), (8, 0, 0, 130), (9, 128, 126, 132), (10, "edge", 254, 130)],
}
DELTAS = [-128, 0, 128, 256, 384, 512]
DELTA_CS = {-128: 0, 0: 0, 128: 126, 256: 254, 384: 382, 512: 510}
DELTA_W = {-128: 2, 0: 130, 128: 132, 256: 132, 384: 130, 512: 2}


def slot_meta(core, j):
    hu = 3 * core + j
    u, half = hu // 2, hu % 2
    return u // 3, u % 3, half, OFFS[half], HALF_LEN[half]


def stage_xk(x, core, j):
    b, s, half, off, hlen = slot_meta(core, j)
    out = np.zeros((KEYS_PAD, D), dtype=np.float32)
    blocks = (off + np.arange(BLOCKS)) % BLOCKS
    pos = 3 * blocks + s
    valid = pos <= L - 1
    out[:BLOCKS][valid] = x[b, pos[valid], :]
    return out


def stage_ccol(core, j):
    _, _, half, off, hlen = slot_meta(core, j)
    c = np.zeros((NQ,), dtype=np.float32)
    q = off + np.arange(hlen)
    c[:hlen] = np.maximum(0, 2 - q) + np.maximum(0, q - (BLOCKS - 3))
    return c


def delta_mask_np(delta, colstart, width):
    r = np.arange(128)[:, None]
    col = colstart + np.arange(width)[None, :]
    return np.where(np.abs(delta + r - col) <= 2, 2.0, 1.0)


def stage_edge_mask(core, j):
    _, _, half, _, _ = slot_meta(core, j)
    m = delta_mask_np(256, 254, 130)
    if half == 1:
        for (slot_, q_) in [(1365, 1363), (1365, 1364), (1366, 1364)]:
            m[slot_ - 1280, q_ - 1278] = 1.0
    return m


def stage_wrap_mask(core, j):
    _, _, half, _, _ = slot_meta(core, j)
    m = np.ones((128, 2))
    if half == 1:
        m[41, 0] = 2.0
        m[42, 0] = 2.0
        m[42, 1] = 2.0
    return m


class Plan:
    ENGINES = ("sync", "gpsimd", "scalar", "vector", "tensor")

    def __init__(self, sems):
        self.ops = {e: [] for e in self.ENGINES}
        self.cnt = {}
        self.sems = sems

    def on(self, eng):
        def deco(fn):
            self.ops[eng].append(fn)
        return deco

    def wait(self, eng, name, val):
        if val > 0:
            s = self.sems[name]
            self.ops[eng].append(lambda E, s=s, v=val: E.wait_ge(s, v))

    def inc(self, name, k=1):
        self.cnt[name] = self.cnt.get(name, 0) + k
        return self.cnt[name]

    def n(self, name):
        return self.cnt.get(name, 0)


class RotBuf:

    def __init__(self, plan, n):
        self.plan, self.nb = plan, n
        self.uses = []
        self.pending = 0

    def acquire(self, eng):
        i = len(self.uses) + self.pending
        if i >= self.nb:
            nm, v = self.uses[i - self.nb]
            self.plan.wait(eng, nm, v)
        self.pending += 1
        return i % self.nb

    def commit(self, semname, val):
        assert self.pending > 0
        self.pending -= 1
        self.uses.append((semname, val))


def build_program(n_slot=N_SLOT, heads=HEADS, do_attn=True, do_ffn=True, n_exp=NE,
                  debug_taps=False):
    nc = bass.Bass("TRN2", target_bir_lowering=False, debug=False)

    xk_d = nc.dram_tensor("xk", [N_SLOT, KEYS_PAD, D], F32, kind="ExternalInput").ap()
    wq_d = nc.dram_tensor("wq", [8, 8, 128, 128], F32, kind="ExternalInput").ap()
    wk_d = nc.dram_tensor("wk", [8, 8, 128, 128], F32, kind="ExternalInput").ap()
    wv_d = nc.dram_tensor("wv65", [8, 128, 1040], F32, kind="ExternalInput").ap()
    gw_d = nc.dram_tensor("gwt", [8, 128, 8], F32, kind="ExternalInput").ap()
    bb_d = nc.dram_tensor("bbias", [128, 8], F32, kind="ExternalInput").ap()
    w1_d = nc.dram_tensor("w1t", [8, 8, 128, DH], F32, kind="ExternalInput").ap()
    w2_d = nc.dram_tensor("w2t", [8, 16, 128, D], F32, kind="ExternalInput").ap()
    cc_d = nc.dram_tensor("ccol", [128, N_SLOT, NQT], F32, kind="ExternalInput").ap()
    me_d = nc.dram_tensor("medge", [N_SLOT, 128, 130], F32, kind="ExternalInput").ap()
    mw_d = nc.dram_tensor("mwrap", [N_SLOT, 128, 2], F32, kind="ExternalInput").ap()

    xq_s = nc.dram_tensor("xnts", [N_SLOT, 8, 128, NQ], BF16, kind="Internal").ap()
    out_d = nc.dram_tensor("out", [N_SLOT, NQ, D], F32, kind="ExternalOutput").ap()
    tap_d = {}
    if debug_taps:
        tap_d["t_xnt"] = nc.dram_tensor("t_xnt", [8, 128, KEYS_PAD], BF16, kind="ExternalOutput").ap()
        tap_d["t_kt"] = nc.dram_tensor("t_kt", [8, 128, KEYS_PAD], BF16, kind="ExternalOutput").ap()
        tap_d["t_qt"] = nc.dram_tensor("t_qt", [8, 128, NQ], BF16, kind="ExternalOutput").ap()
        tap_d["t_v65"] = nc.dram_tensor("t_v65", [KT_N, 128, 1040], BF16, kind="ExternalOutput").ap()
        tap_d["t_attn"] = nc.dram_tensor("t_attn", [NQ, D], F32, kind="ExternalOutput").ap()
        tap_d["t_comb"] = nc.dram_tensor("t_comb", [128, NQT, 8], F32, kind="ExternalOutput").ap()
        tap_d["t_stats"] = nc.dram_tensor("t_stats", [128, 16], F32, kind="ExternalOutput").ap()
        tap_d["t_xn"] = nc.dram_tensor("t_xn", [128, 2, D], BF16, kind="ExternalOutput").ap()
        tap_d["t_xt"] = nc.dram_tensor("t_xt", [128, 2, D], F32, kind="ExternalOutput").ap()

    mask_np = np.ones((128, 6, 132), dtype=BFNP)
    for i, dl in enumerate(DELTAS):
        w = DELTA_W[dl]
        mask_np[:, i, :w] = delta_mask_np(dl, DELTA_CS[dl], w).astype(BFNP)
    maskd_dram = nc.inline_tensor(np.ascontiguousarray(mask_np), name="maskd").ap()
    idf_dram = nc.inline_tensor(np.eye(128, dtype=np.float32), name="idf").ap()
    idb_dram = nc.inline_tensor(np.eye(128, dtype=BFNP), name="idb").ap()
    eps_dram = nc.inline_tensor(np.full((128, 1), 1e-5, dtype=np.float32), name="epsc").ap()

    es = ExitStack()

    def sb(name, shape, dtype):
        return es.enter_context(nc.sbuf_tensor(name, shape, dtype))

    def psumt(name, shape, dtype):
        return es.enter_context(nc.psum_tensor(name, shape, dtype))

    XnT = sb("XnT", [128, 8, KEYS_PAD], BF16)
    KT = sb("KT", [128, 8, KEYS_PAD], BF16)
    V65 = sb("V65", [128, KT_N, 1040], BF16)
    QT = sb("QT", [128, 8, NQ], BF16)
    UU = sb("UU", [128, 8320], BF16)
    x_tile = sb("x_tile", [128, 2, D], F32)
    xn_tile = sb("xn_tile", [128, 2, D], BF16)
    w_slab = sb("w_slab", [128, 2, 8, 128], BF16)
    pt_buf = sb("pt_buf", [128, 3, 1024], BF16)
    ot_stage = sb("ot_stage", [128, 2, 512], F32)
    xsq = ot_stage[:].rearrange("p a b -> p (a b)")
    stats = sb("stats", [128, 16], F32)
    ccol = sb("ccol_s", [128, N_SLOT, NQT], F32)
    maskd = sb("maskd_s", [128, 6, 132], BF16)
    medge = sb("medge_s", [128, N_SLOT, 130], BF16)
    mwrap = sb("mwrap_s", [128, N_SLOT, 2], BF16)
    identf = sb("identf", [128, 128], F32)
    identb = sb("identb", [128, 128], BF16)
    biasb = sb("biasb", [128, 8], F32)
    epsc = sb("epsc_s", [128, 1], F32)
    gwt = sb("gwt_s", [128, 8, 8], BF16)
    gbuf = sb("gbuf", [128, 6, 8], F32)
    gsml = sb("gsml", [128, 8], F32)
    comb = sb("comb", [128, N_SLOT * NQT, 8], F32)

    attn_out = UU[:].bitcast(F32)[:, 0:4096].rearrange("p (a b) -> p a b", a=4)
    wv65_sb = UU[:].rearrange("p (a b) -> p a b", a=8)
    Hsb = UU[:][:, 0:8192].rearrange("p (a b) -> p a b", a=16)
    w1A = XnT[:][:, :, 0:DH]
    w1B = KT[:][:, :, 0:DH]
    w2S = V65[:][:, 0:16, 0:D]
    xnq_sb = QT[:]
    eo_stage = x_tile

    ps_bigF = [psumt("ps_bigF0", [128, 1024], F32), psumt("ps_bigF1", [128, 1024], F32),
               psumt("ps_bigF2", [128, 1024], F32)]
    ps_mm = [b[:, c:c + 512] for b in ps_bigF for c in (0, 512)]
    ps_bigS = [b[:] for b in ps_bigF]
    ps_o = [psumt("ps_o0", [128, 512], F32), psumt("ps_o1", [128, 512], F32)]
    ps_t = [ps_o[0][:].bitcast(BF16), ps_o[1][:].bitcast(BF16)]
    Dm = sb("Dm", [128, NQT, 16], F32)
    dche = sb("dche", [128, 16], F32)
    drec = sb("drec", [128, 16], F32)

    sem_names = ["init_s", "init_g", "xload", "sq", "var", "rstd", "xn", "tpd",
                 "xnt", "wload", "kqe", "v65e", "v65s", "ones", "gmm", "gate_v",
                 "gate_s", "combr", "sS", "sP", "sB", "sOm", "sOe", "sTp", "sOn", "dme",
                 "aout", "xq", "xacc", "so", "qstore", "fenceT", "cw", "cxq", "cH", "cSi",
                 "cEOg", "cEO", "cEv", "cAcc", "tapd"]
    sems = {nm: es.enter_context(nc.semaphore(nm)) for nm in sem_names}
    P = Plan(sems)
    Se = sems

    mm_rb = RotBuf(P, 6)
    s_rb = RotBuf(P, 3)
    po_rb = RotBuf(P, 2)
    xt_rb = RotBuf(P, 2)
    xnt_rb = RotBuf(P, 2)
    ws_rb = RotBuf(P, 2)
    ot_rb = RotBuf(P, 2)
    hs_rb = RotBuf(P, 1)
    w1_rb = RotBuf(P, 2)
    w2_rb = RotBuf(P, 1)
    xq_rb = RotBuf(P, 1)

    def p_init():
        @P.on("sync")
        def _(E):
            E.dma_start(out=ccol[:], in_=cc_d[:]).then_inc(Se["init_s"], 16)
            E.dma_start(out=identf[:], in_=idf_dram[:]).then_inc(Se["init_s"], 16)
            E.dma_start(out=identb[:], in_=idb_dram[:]).then_inc(Se["init_s"], 16)
            E.dma_start(out=biasb[:], in_=bb_d[:]).then_inc(Se["init_s"], 16)
            E.dma_start(out=epsc[:], in_=eps_dram[:]).then_inc(Se["init_s"], 16)
        P.inc("init_s", 5 * 16)

        @P.on("gpsimd")
        def _(E):
            E.dma_start(out=maskd[:], in_=maskd_dram[:]).then_inc(Se["init_g"], 16)
            E.dma_start(out=medge[:], in_=me_d.rearrange("a p b -> p a b")).then_inc(Se["init_g"], 16)
            E.dma_start(out=mwrap[:], in_=mw_d.rearrange("a p b -> p a b")).then_inc(Se["init_g"], 16)
            E.dma_start(out=gwt[:], in_=gw_d.rearrange("a p b -> p a b")).then_inc(Se["init_g"], 16)
        P.inc("init_g", 4 * 16)
        for e in Plan.ENGINES:
            P.wait(e, "init_s", P.n("init_s"))
            P.wait(e, "init_g", P.n("init_g"))

    def p1(slot):
        for kt in range(KT_N):
            xp = xt_rb.acquire("sync")
            xl = P.inc("xload", 16)

            @P.on("sync")
            def _(E, sl=slot, k=kt, p=xp):
                E.dma_start(out=x_tile[:, p, :],
                            in_=xk_d[sl, k * 128:(k + 1) * 128, :]).then_inc(Se["xload"], 16)

            P.wait("scalar", "xload", xl)
            if P.n("sq") >= 2:
                P.wait("scalar", "var", P.n("sq") - 1)
            sq = P.inc("sq")

            @P.on("scalar")
            def _(E, p=xp):
                E.activation(out=xsq, in_=x_tile[:, p, :], func=AF.Square,
                             accum_out=stats[:, 8 + p:9 + p]).then_inc(Se["sq"], 1)

            P.wait("vector", "xload", xl)

            @P.on("vector")
            def _(E, p=xp):
                E.reduce_sum(stats[:, 0 + p:1 + p], x_tile[:, p, :], axis=AX)
                E.drain()
                E.tensor_scalar(out=stats[:, 2 + p:3 + p], in0=stats[:, 0 + p:1 + p],
                                scalar1=-1.0 / D, scalar2=None, op0=ALU.mult)
                E.drain()
                E.tensor_mul(stats[:, 4 + p:5 + p], stats[:, 2 + p:3 + p], stats[:, 2 + p:3 + p])
                E.drain()
            P.wait("vector", "sq", sq)
            if P.n("var") >= 2:
                P.wait("vector", "rstd", P.n("var") - 1)
            vv = P.inc("var")

            @P.on("vector")
            def _(E, p=xp):
                E.scalar_tensor_tensor(out=stats[:, 6 + p:7 + p], in0=stats[:, 8 + p:9 + p],
                                       scalar=1.0 / D, in1=stats[:, 4 + p:5 + p],
                                       op0=ALU.mult, op1=ALU.subtract).then_inc(Se["var"], 1)

            P.wait("scalar", "var", vv)
            if P.n("rstd") >= 2:
                P.wait("scalar", "xn", P.n("rstd") - 1)
            rs = P.inc("rstd")

            @P.on("scalar")
            def _(E, p=xp):
                E.activation(out=stats[:, 10 + p:11 + p], in_=stats[:, 6 + p:7 + p],
                             func=AF.Ln, bias=epsc[:])
                E.drain()
                E.activation(out=stats[:, 12 + p:13 + p], in_=stats[:, 10 + p:11 + p],
                             func=AF.Exp, scale=-0.5).then_inc(Se["rstd"], 1)

            P.wait("vector", "rstd", rs)
            np_ = xnt_rb.acquire("vector")
            xv = P.inc("xn")

            @P.on("vector")
            def _(E, p=xp, q=np_):
                E.tensor_scalar(out=xn_tile[:, q, :], in0=x_tile[:, p, :],
                                scalar1=stats[:, 2 + p:3 + p], scalar2=stats[:, 12 + p:13 + p],
                                op0=ALU.add, op1=ALU.mult).then_inc(Se["xn"], 1)
            xt_rb.commit("xn", xv)

            P.wait("sync", "xn", xv)
            xe = P.inc("xnt", 8 * 16)

            @P.on("sync")
            def _(E, p=np_, k=kt):
                for dk in range(8):
                    E.dma_start(out=XnT[:, dk, k * 128:(k + 1) * 128],
                                in_=xn_tile[:, p, dk * 128:(dk + 1) * 128],
                                transpose=True).then_inc(Se["xnt"], 16)
            xnt_rb.commit("xnt", xe)

    def p2(slot):
        xnt_done = P.n("xnt")
        P.wait("sync", "xnt", xnt_done)
        qs = P.inc("qstore", 16)

        @P.on("sync")
        def _(E, sl=slot):
            E.dma_start(out=xq_s[sl].rearrange("a p b -> p a b"),
                        in_=XnT[:, :, 0:NQ]).then_inc(Se["qstore"], 16)

        P.wait("gpsimd", "so", P.n("so"))
        wv = P.inc("wload", 16)

        @P.on("gpsimd")
        def _(E):
            E.dma_start(out=wv65_sb, in_=wv_d.rearrange("a p b -> p a b")).then_inc(Se["wload"], 16)

        P.wait("tensor", "sP", P.n("sP"))

        def qk_proj(w_dram, dst, n_total):
            for do in range(8):
                sp_ = ws_rb.acquire("gpsimd")
                wl = P.inc("wload", 16)

                @P.on("gpsimd")
                def _(E, a=w_dram[do], p=sp_):
                    E.dma_start(out=w_slab[:, p, :, :],
                                in_=a.rearrange("a p b -> p a b")).then_inc(Se["wload"], 16)
                nch = [(i * 512, min(512, n_total - i * 512))
                       for i in range((n_total + 511) // 512)]
                last_ss = 0
                for (n0, nw) in nch:
                    mp = mm_rb.acquire("tensor")
                    P.wait("tensor", "wload", wl)
                    P.wait("tensor", "xnt", xnt_done)
                    ss = P.inc("sS")
                    last_ss = ss

                    @P.on("tensor")
                    def _(E, p=sp_, q=mp, a=n0, w=nw):
                        for dk in range(7):
                            E.matmul(ps_mm[q][:, 0:w], lhsT=w_slab[:, p, dk, :],
                                     rhs=XnT[:, dk, a:a + w], start=(dk == 0), stop=False)
                        E.matmul(ps_mm[q][:, 0:w], lhsT=w_slab[:, p, 7, :],
                                 rhs=XnT[:, 7, a:a + w], start=False, stop=True).then_inc(Se["sS"], 1)
                    P.wait("scalar", "sS", ss)
                    kq = P.inc("kqe")

                    @P.on("scalar")
                    def _(E, q=mp, d=do, a=n0, w=nw, t=dst):
                        E.activation(out=t[:, d, a:a + w], in_=ps_mm[q][:, 0:w],
                                     func=AF.Copy).then_inc(Se["kqe"], 1)
                    mm_rb.commit("kqe", kq)
                ws_rb.commit("sS", last_ss)

        qk_proj(wq_d, QT[:], NQ)
        qk_proj(wk_d, KT[:], KEYS_PAD)

        P.wait("tensor", "wload", wv)
        for kt in range(KT_N):
            for ch in range(4):
                vp = po_rb.acquire("tensor")
                ve = P.inc("v65e")

                @P.on("tensor")
                def _(E, k=kt, c=ch, q=vp):
                    for dk in range(7):
                        E.matmul(ps_o[q][:, 0:260], lhsT=XnT[:, dk, k * 128:(k + 1) * 128],
                                 rhs=wv65_sb[:, dk, c * 260:(c + 1) * 260],
                                 start=(dk == 0), stop=False)
                    E.matmul(ps_o[q][:, 0:260], lhsT=XnT[:, 7, k * 128:(k + 1) * 128],
                             rhs=wv65_sb[:, 7, c * 260:(c + 1) * 260],
                             start=False, stop=True).then_inc(Se["v65e"], 1)
                P.wait("scalar", "v65e", ve)
                vs = P.inc("v65s")

                @P.on("scalar")
                def _(E, k=kt, c=ch, q=vp):
                    E.activation(out=V65[:, k, c * 260:(c + 1) * 260], in_=ps_o[q][:, 0:260],
                                 func=AF.Copy).then_inc(Se["v65s"], 1)
                po_rb.commit("v65s", vs)
        P.wait("vector", "v65s", P.n("v65s"))

        @P.on("vector")
        def _(E):
            for k in range(KT_N - 1):
                E.memset(V65[:, k, 64:1040:65], 1.0)
            E.memset(V65[0:43, KT_N - 1, 64:1040:65], 1.0).then_inc(Se["ones"], 1)
        P.inc("ones")

        for tt in range(NQT):
            mp = mm_rb.acquire("tensor")
            gm = P.inc("gmm")

            @P.on("tensor")
            def _(E, t=tt, q=mp):
                for dk in range(7):
                    E.matmul(ps_mm[q][:, 0:8], lhsT=XnT[:, dk, t * 128:(t + 1) * 128],
                             rhs=gwt[:, dk, :], start=(dk == 0), stop=False)
                E.matmul(ps_mm[q][:, 0:8], lhsT=XnT[:, 7, t * 128:(t + 1) * 128],
                         rhs=gwt[:, 7, :], start=False, stop=True).then_inc(Se["gmm"], 1)
            P.wait("vector", "gmm", gm)
            gv = P.inc("gate_v")

            @P.on("vector")
            def _(E, q=mp):
                E.tensor_tensor(gbuf[:, 0, :], ps_mm[q][:, 0:8], biasb[:],
                                op=ALU.add).then_inc(Se["gate_v"], 1)
            mm_rb.commit("gate_v", gv)
            P.wait("scalar", "gate_v", gv)
            gs = P.inc("gate_s")

            @P.on("scalar")
            def _(E):
                E.activation(out=gbuf[:, 1, :], in_=gbuf[:, 0, :],
                             func=AF.Sigmoid).then_inc(Se["gate_s"], 1)
            P.wait("vector", "gate_s", gs)
            ci = slot * NQT + tt
            cr = P.inc("combr")

            @P.on("vector")
            def _(E, c=ci):
                g2 = gbuf[:, 1, :]
                E.reduce_max(gsml[:, 0:1], g2, axis=AX)
                E.drain()
                E.tensor_scalar(out=gbuf[:, 2, :], in0=g2, scalar1=gsml[:, 0:1],
                                scalar2=None, op0=ALU.is_ge)
                E.drain()
                E.scalar_tensor_tensor(out=gbuf[:, 3, :], in0=gbuf[:, 2, :], scalar=-1e9,
                                       in1=g2, op0=ALU.mult, op1=ALU.add)
                E.drain()
                E.reduce_max(gsml[:, 1:2], gbuf[:, 3, :], axis=AX)
                E.drain()
                E.tensor_scalar(out=gbuf[:, 4, :], in0=g2, scalar1=gsml[:, 1:2],
                                scalar2=None, op0=ALU.is_ge)
                E.drain()
                E.tensor_mul(gbuf[:, 5, :], g2, gbuf[:, 4, :])
                E.tensor_add(gsml[:, 2:3], gsml[:, 0:1], gsml[:, 1:2])
                E.drain()
                E.tensor_scalar(out=gsml[:, 2:3], in0=gsml[:, 2:3], scalar1=1e-6,
                                scalar2=None, op0=ALU.add)
                E.drain()
                E.reciprocal(gsml[:, 3:4], gsml[:, 2:3])
                E.drain()
                E.tensor_scalar(out=comb[:, c, :], in0=gbuf[:, 5, :], scalar1=gsml[:, 3:4],
                                scalar2=None, op0=ALU.mult).then_inc(Se["combr"], 1)

    def p3(slot):
        v65_ready = P.n("v65s")
        ones_ready = P.n("ones")
        kq_ready = P.n("kqe")
        P.wait("tensor", "v65s", v65_ready)
        P.wait("tensor", "ones", ones_ready)
        P.wait("tensor", "kqe", kq_ready)
        P.wait("tensor", "gate_v", P.n("gate_v"))
        for qci, (q0, qw, ntl) in enumerate(QCL):
            bops = {}
            for (kt, kind, cs, wd) in BAND_OPS[qci]:
                bops.setdefault(kt, []).append((kind, cs, wd))
            deferred = []
            for hp in range(heads // 2):
                h0, h1 = 2 * hp, 2 * hp + 1
                o0 = po_rb.acquire("tensor")
                o1 = po_rb.acquire("tensor")
                pend = []

                def emit_O_pair(fin):
                    k, t, ls, lv = pend.pop(0)
                    P.wait("tensor", ls, lv)
                    last = fin and not pend
                    if last:
                        som_ = P.inc("sOm")

                    @P.on("tensor")
                    def _(E, k=k, t=t, w=qw, A=h0, Bh=h1, a=o0, b=o1, last=last):
                        E.matmul(ps_o[a][0:65, 0:w], lhsT=V65[:, k, 65 * A:65 * A + 65],
                                 rhs=pt_buf[:, t, 0:w], start=(k == 0), stop=(k == KT_N - 1))
                        mm = E.matmul(ps_o[b][0:65, 0:w], lhsT=V65[:, k, 65 * Bh:65 * Bh + 65],
                                      rhs=pt_buf[:, t, 512:512 + w],
                                      start=(k == 0), stop=(k == KT_N - 1))
                        if last:
                            mm.then_inc(Se["sOm"], 1)
                    return P.n("sOm") if last else None

                for kt in range(KT_N):
                    sq_ = s_rb.acquire("tensor")
                    ss = P.inc("sS")

                    @P.on("tensor")
                    def _(E, k=kt, q=sq_, a=q0, w=qw, HP=hp):
                        E.matmul(ps_bigS[q][:, 0:w],
                                 lhsT=KT[0:64, HP, k * 128:(k + 1) * 128],
                                 rhs=QT[0:64, HP, a:a + w],
                                 start=True, stop=True, tile_position=(0, 0))
                        E.matmul(ps_bigS[q][:, 512:512 + w],
                                 lhsT=KT[64:128, HP, k * 128:(k + 1) * 128],
                                 rhs=QT[64:128, HP, a:a + w],
                                 start=True, stop=True,
                                 tile_position=(64, 0)).then_inc(Se["sS"], 1)
                    if kt == 1 and deferred:
                        for fn in deferred:
                            fn()
                        deferred = []
                    if pend:
                        emit_O_pair(False)
                    P.wait("scalar", "sS", ss)
                    sp = P.inc("sP")
                    ptp = (sp - 1) % 3

                    @P.on("scalar")
                    def _(E, q=sq_, t=ptp, fd=512 + qw):
                        E.activation(out=pt_buf[:, t, 0:fd], in_=ps_bigS[q][:, 0:fd],
                                     func=AF.Exp, scale=0.125).then_inc(Se["sP"], 1)
                    s_rb.commit("sP", sp)
                    last_sem, last_val = "sP", sp
                    if kt in bops:
                        P.wait("vector", "sP", sp)
                        for (kind, cs, wd) in bops[kt]:
                            if kind == "edge":
                                msrc = medge[:, slot, 0:wd]
                            elif kind == "wrap":
                                msrc = mwrap[:, slot, 0:wd]
                            else:
                                msrc = maskd[:, DELTAS.index(kind), 0:wd]
                            P.inc("sB")

                            @P.on("vector")
                            def _(E, t=ptp, c=cs, w=wd, m=msrc):
                                E.tensor_mul(pt_buf[:, t, c:c + w], pt_buf[:, t, c:c + w], m)
                            P.inc("sB")

                            @P.on("vector")
                            def _(E, t=ptp, c=512 + cs, w=wd, m=msrc):
                                E.tensor_mul(pt_buf[:, t, c:c + w], pt_buf[:, t, c:c + w],
                                             m).then_inc(Se["sB"], 2)
                        last_sem, last_val = "sB", P.n("sB")
                    pend.append((kt, ptp, last_sem, last_val))
                while len(pend) > 1:
                    emit_O_pair(False)
                som = emit_O_pair(True)
                P.wait("vector", "sOm", som)
                ot0 = ot_rb.acquire("vector")
                ot1 = ot_rb.acquire("vector")
                soe = P.inc("sOe")

                @P.on("vector")
                def _(E, a=o0, b=o1, t0=ot0, t1=ot1, w=qw):
                    E.tensor_copy(out=ot_stage[:, t0, 0:w], in_=ps_o[a][:, 0:w])
                    E.tensor_copy(out=ot_stage[:, t1, 0:w],
                                  in_=ps_o[b][:, 0:w]).then_inc(Se["sOe"], 1)
                po_rb.commit("sOe", soe)
                po_rb.commit("sOe", soe)

                def make_tail(h0=h0, h1=h1, ot0=ot0, ot1=ot1, soe=soe, n=ntl, t0q=q0 // 128):
                    def tail():
                        P.wait("tensor", "sOe", soe)
                        trq = s_rb.acquire("tensor")
                        stp = P.inc("sTp")

                        @P.on("tensor")
                        def _(E, q=trq):
                            for par, ot in ((0, ot0), (1, ot1)):
                                for c in range(n):
                                    E.transpose(ps_bigS[q][:, 256 * par + c * 64:256 * par + (c + 1) * 64],
                                                ot_stage[0:64, ot, c * 128:(c + 1) * 128],
                                                identf[0:64, 0:64])
                            for par, ot in ((0, ot0), (1, ot1)):
                                for c in range(n):
                                    mm = E.transpose(
                                        ps_bigS[q][:, 512 + 8 * par + c:512 + 8 * par + c + 1],
                                        ot_stage[64:65, ot, c * 128:(c + 1) * 128],
                                        identf[64:65, 64:65])
                            mm.then_inc(Se["sTp"], 1)
                        ot_rb.commit("sTp", stp)
                        ot_rb.commit("sTp", stp)
                        P.wait("vector", "sTp", stp)
                        son = P.inc("sOn")

                        @P.on("vector")
                        def _(E, q=trq):
                            E.tensor_copy(out=attn_out[:, 0:n, 64 * h0:64 * h0 + 64],
                                          in_=ps_bigS[q][:, 0:n * 64].rearrange("p (a b) -> p a b", b=64))
                            E.tensor_copy(out=attn_out[:, 0:n, 64 * h1:64 * h1 + 64],
                                          in_=ps_bigS[q][:, 256:256 + n * 64].rearrange("p (a b) -> p a b", b=64))
                            E.tensor_copy(out=Dm[:, t0q:t0q + n, h0:h0 + 1],
                                          in_=ps_bigS[q][:, 512:512 + n].rearrange("p (a b) -> p a b", b=1))
                            E.tensor_copy(out=Dm[:, t0q:t0q + n, h1:h1 + 1],
                                          in_=ps_bigS[q][:, 520:520 + n].rearrange("p (a b) -> p a b", b=1)) \
                                .then_inc(Se["sOn"], 1)
                        s_rb.commit("sOn", son)
                    return tail
                deferred.append(make_tail())
            for fn in deferred:
                fn()
            deferred = []
            if debug_taps and slot == 0:
                P.wait("sync", "sOn", P.n("sOn"))
                for tl in range(ntl):
                    td = P.inc("tapd", 16)

                    @P.on("sync")
                    def _(E, t=q0 // 128 + tl, c=tl):
                        E.dma_start(out=tap_d["t_attn"][t * 128:(t + 1) * 128, :],
                                    in_=attn_out[:, c, :]).then_inc(Se["tapd"], 16)
                P.wait("vector", "tapd", P.n("tapd"))
            for tl in range(ntl):
                tt_g = q0 // 128 + tl
                ao = P.inc("aout")

                @P.on("vector")
                def _(E, c=tl, t=tt_g, sl=slot):
                    E.tensor_scalar(out=dche[:], in0=Dm[:, t, :],
                                    scalar1=ccol[:, sl, t:t + 1], scalar2=None, op0=ALU.add)
                    E.drain()
                    E.reciprocal(drec[:], dche[:])
                    E.drain()
                    for H in range(15):
                        E.tensor_scalar(out=attn_out[:, c, 64 * H:64 * H + 64],
                                        in0=attn_out[:, c, 64 * H:64 * H + 64],
                                        scalar1=drec[:, H:H + 1], scalar2=None, op0=ALU.mult)
                    E.tensor_scalar(out=attn_out[:, c, 960:1024],
                                    in0=attn_out[:, c, 960:1024],
                                    scalar1=drec[:, 15:16], scalar2=None,
                                    op0=ALU.mult).then_inc(Se["aout"], 1)
                P.wait("sync", "aout", ao)
                so = P.inc("so", 16)

                @P.on("sync")
                def _(E, sl=slot, t=tt_g, c=tl):
                    E.dma_start(out=out_d[sl, t * 128:(t + 1) * 128, :],
                                in_=attn_out[:, c, :]).then_inc(Se["so"], 16)
            P.wait("vector", "so", P.n("so"))
        P.wait("gpsimd", "so", P.n("so"))
        P.inc("xacc", 16)

        P.inc("xacc", 16)

        @P.on("gpsimd")
        def _(E, sl=slot):
            E.dma_start(out=out_d[sl, 0:NQ:2, :], in_=xk_d[sl, 0:NQ:2, :],
                        accum_op=ALU.add).then_inc(Se["xacc"], 16)
            E.dma_start(out=out_d[sl, 1:NQ:2, :], in_=xk_d[sl, 1:NQ:2, :],
                        accum_op=ALU.add).then_inc(Se["xacc"], 16)

    def pC():
        P.wait("gpsimd", "sOn", P.n("sOn"))
        P.wait("gpsimd", "sP", P.n("sP"))
        P.wait("tensor", "sP", P.n("sP"))
        P.wait("gpsimd", "so", P.n("so"))
        P.wait("scalar", "so", P.n("so"))
        P.wait("scalar", "combr", P.n("combr"))
        P.wait("sync", "sOn", P.n("sOn"))

        w1bufs = [w1A, w1B]

        def load_w1(e):
            wp = w1_rb.acquire("gpsimd")
            cw = P.inc("cw", 16)

            @P.on("gpsimd")
            def _(E, a=w1_d[e], p=wp):
                E.dma_start(out=w1bufs[p], in_=a.rearrange("a p b -> p a b")).then_inc(Se["cw"], 16)
            return wp, cw

        def load_w2(e):
            w2_rb.acquire("gpsimd")
            cw = P.inc("cw", 16)

            @P.on("gpsimd")
            def _(E, a=w2_d[e]):
                E.dma_start(out=w2S, in_=a.rearrange("a p b -> p a b")).then_inc(Se["cw"], 16)
            return cw

        w1p, w1c = load_w1(0)
        w2c = load_w2(0)
        for e in range(n_exp):
            if e + 1 < n_exp:
                nxt = load_w1(e + 1)
            for slot in range(n_slot):
                xqp = xq_rb.acquire("sync")
                P.wait("sync", "qstore", P.n("qstore"))
                cq = P.inc("cxq", 16)

                @P.on("sync")
                def _(E, sl=slot):
                    E.dma_start(out=xnq_sb, in_=xq_s[sl].rearrange("a p b -> p a b")) \
                        .then_inc(Se["cxq"], 16)
                last_H = 0
                last_EO = 0
                for (t0, tw, ntl) in QCL:
                    hsp = hs_rb.acquire("scalar")
                    P.wait("tensor", "cw", w1c)
                    P.wait("tensor", "cxq", cq)
                    for ht in range(16):
                        mp = mm_rb.acquire("tensor")
                        ch = P.inc("cH")
                        last_H = ch

                        @P.on("tensor")
                        def _(E, p=w1p, q=mp, H=ht, a=t0, w=tw):
                            for dk in range(7):
                                E.matmul(ps_mm[q][:, 0:w], lhsT=w1bufs[p][:, dk, H * 128:(H + 1) * 128],
                                         rhs=xnq_sb[:, dk, a:a + w], start=(dk == 0), stop=False)
                            E.matmul(ps_mm[q][:, 0:w], lhsT=w1bufs[p][:, 7, H * 128:(H + 1) * 128],
                                     rhs=xnq_sb[:, 7, a:a + w], start=False,
                                     stop=True).then_inc(Se["cH"], 1)
                        P.wait("scalar", "cH", ch)
                        cs = P.inc("cSi")

                        @P.on("scalar")
                        def _(E, q=mp, H=ht, w=tw):
                            E.activation(out=Hsb[:, H, 0:w], in_=ps_mm[q][:, 0:w],
                                         func=AF.Silu).then_inc(Se["cSi"], 1)
                        mm_rb.commit("cSi", cs)
                    P.wait("tensor", "cSi", P.n("cSi"))
                    P.wait("tensor", "cw", w2c)
                    for tl in range(ntl):
                        tt_g = t0 // 128 + tl
                        pb0 = po_rb.acquire("tensor")
                        pb1 = mm_rb.acquire("tensor")
                        banks = [ps_o[pb0][:], ps_mm[pb1]]
                        ceog = 0
                        for dh in range(2):
                            ce = P.inc("cEOg")
                            ceog = ce

                            @P.on("tensor")
                            def _(E, bk=banks[dh], c=tl, d=dh, w=512):
                                for ht in range(15):
                                    E.matmul(bk[:, 0:w], lhsT=Hsb[:, ht, c * 128:(c + 1) * 128],
                                             rhs=w2S[:, ht, d * 512:(d + 1) * 512],
                                             start=(ht == 0), stop=False)
                                E.matmul(bk[:, 0:w], lhsT=Hsb[:, 15, c * 128:(c + 1) * 128],
                                         rhs=w2S[:, 15, d * 512:(d + 1) * 512],
                                         start=False, stop=True).then_inc(Se["cEOg"], 1)
                        last_EO = ceog
                        P.wait("scalar", "cEOg", ceog)
                        ep = xt_rb.acquire("scalar")
                        cv = P.inc("cEv")

                        @P.on("scalar")
                        def _(E, b0=banks[0], b1=banks[1], p=ep, sl=slot, t=tt_g, ee=e):
                            E.activation(out=eo_stage[:, p, 0:512], in_=b0[:, 0:512],
                                         func=AF.Copy, scale=comb[:, sl * NQT + t, ee:ee + 1])
                            E.activation(out=eo_stage[:, p, 512:1024], in_=b1[:, 0:512],
                                         func=AF.Copy,
                                         scale=comb[:, sl * NQT + t, ee:ee + 1]).then_inc(Se["cEv"], 1)
                        po_rb.commit("cEv", cv)
                        mm_rb.commit("cEv", cv)
                        P.wait("gpsimd", "cEv", cv)
                        ca = P.inc("cAcc", 16)

                        @P.on("gpsimd")
                        def _(E, p=ep, sl=slot, t=tt_g):
                            E.dma_start(out=out_d[sl, t * 128:(t + 1) * 128, :],
                                        in_=eo_stage[:, p, :],
                                        accum_op=ALU.add).then_inc(Se["cAcc"], 16)
                        xt_rb.commit("cAcc", ca)
                    hs_rb.commit("cEOg", last_EO)
                xq_rb.commit("cH", last_H)
            w1_rb.commit("cH", last_H)
            w2_rb.commit("cEOg", last_EO)
            if e + 1 < n_exp:
                w1p, w1c = nxt
                w2c = load_w2(e + 1)
        P.wait("gpsimd", "cAcc", P.n("cAcc"))
        P.wait("gpsimd", "xacc", P.n("xacc"))

    def p_taps_p2():
        P.wait("sync", "xnt", P.n("xnt"))
        for _ in range(3):
            P.inc("tapd", 16)

        @P.on("sync")
        def _(E):
            E.dma_start(out=tap_d["t_stats"], in_=stats[:]).then_inc(Se["tapd"], 16)
            E.dma_start(out=tap_d["t_xn"], in_=xn_tile[:]).then_inc(Se["tapd"], 16)
            E.dma_start(out=tap_d["t_xt"], in_=x_tile[:]).then_inc(Se["tapd"], 16)
        P.wait("sync", "kqe", P.n("kqe"))
        P.wait("sync", "v65s", P.n("v65s"))
        P.wait("sync", "ones", P.n("ones"))
        P.wait("sync", "combr", P.n("combr"))
        for _ in range(5):
            P.inc("tapd", 16)

        @P.on("sync")
        def _(E):
            E.dma_start(out=tap_d["t_xnt"].rearrange("a p b -> p a b"),
                        in_=XnT[:]).then_inc(Se["tapd"], 16)
            E.dma_start(out=tap_d["t_kt"].rearrange("a p b -> p a b"),
                        in_=KT[:]).then_inc(Se["tapd"], 16)
            E.dma_start(out=tap_d["t_qt"].rearrange("a p b -> p a b"),
                        in_=QT[:]).then_inc(Se["tapd"], 16)
            E.dma_start(out=tap_d["t_v65"].rearrange("a p b -> p a b"),
                        in_=V65[:]).then_inc(Se["tapd"], 16)
            E.dma_start(out=tap_d["t_comb"], in_=comb[:, 0:NQT, :]).then_inc(Se["tapd"], 16)

    p_init()
    for slot in range(n_slot):
        p1(slot)
        p2(slot)
        if debug_taps and slot == 0:
            p_taps_p2()
        if do_attn:
            p3(slot)
    if do_ffn:
        pC()
    else:
        P.wait("sync", "so", P.n("so"))
    if debug_taps:
        P.wait("sync", "tapd", P.n("tapd"))

    blk = es.enter_context(nc.Block())
    for eng in Plan.ENGINES:
        def body(E, ops=tuple(P.ops[eng])):
            for fn in ops:
                fn(E)
        getattr(blk, eng)(body)
    es.close()
    return nc


def stage_core_inputs(core, x, W_q, W_k, W_v, gate_w, balance_bias, w1, w2, shared):
    xk = np.stack([stage_xk(x, core, j) for j in range(N_SLOT)])
    cc = np.stack([stage_ccol(core, j) for j in range(N_SLOT)])
    ccol = np.ascontiguousarray(cc.reshape(N_SLOT, NQT, 128).transpose(2, 0, 1)).astype(np.float32)
    medge = np.stack([stage_edge_mask(core, j) for j in range(N_SLOT)]).astype(np.float32)
    mwrap = np.stack([stage_wrap_mask(core, j) for j in range(N_SLOT)]).astype(np.float32)
    m = dict(shared)
    m.update(xk=np.ascontiguousarray(xk), ccol=ccol,
             medge=np.ascontiguousarray(medge), mwrap=np.ascontiguousarray(mwrap))
    return m


def make_shared(W_q, W_k, W_v, gate_w, balance_bias, w1, w2):
    wq = np.ascontiguousarray(W_q.reshape(8, 128, 8, 128).transpose(2, 0, 1, 3))
    wk = np.ascontiguousarray(W_k.reshape(8, 128, 8, 128).transpose(2, 0, 1, 3))
    wv65 = np.zeros((1024, 1040), dtype=np.float32)
    for h in range(HEADS):
        wv65[:, 65 * h:65 * h + 64] = W_v[:, 64 * h:64 * h + 64]
    wv65 = np.ascontiguousarray(wv65.reshape(8, 128, 1040))
    gwt = np.ascontiguousarray(gate_w.T.reshape(8, 128, 8))
    bb = np.ascontiguousarray(np.tile(balance_bias.reshape(1, 8), (128, 1))).astype(np.float32)
    w1t = np.ascontiguousarray(w1.transpose(0, 2, 1).reshape(8, 8, 128, DH))
    w2t = np.ascontiguousarray(w2.transpose(0, 2, 1).reshape(8, 16, 128, D))
    return dict(wq=wq, wk=wk, wv65=wv65, gwt=gwt, bbias=bb, w1t=w1t, w2t=w2t)


def kernel(L0_attn_x, L0_attn_W_q, L0_attn_W_k, L0_attn_W_v,
           L0_ffn_gate_w, L0_ffn_balance_bias, L0_ffn_w1, L0_ffn_w2):
    x = np.asarray(L0_attn_x, dtype=np.float32)
    args = [np.asarray(a, dtype=np.float32) for a in
            (L0_attn_W_q, L0_attn_W_k, L0_attn_W_v, L0_ffn_gate_w,
             L0_ffn_balance_bias, L0_ffn_w1, L0_ffn_w2)]
    shared = make_shared(*args)
    nc = build_program()
    in_maps = [stage_core_inputs(c, x, *args, shared) for c in range(8)]
    res = run_bass_kernel_spmd(nc, in_maps, core_ids=list(range(8)))
    out = np.zeros((B, L, D), dtype=np.float32)
    for c in range(8):
        oc = res.results[c]["out"]
        for j in range(N_SLOT):
            b, s, half, off, hlen = slot_meta(c, j)
            i = np.arange(hlen)
            pos = 3 * (off + i) + s
            keep = pos <= L - 1
            out[b, pos[keep], :] = oc[j, i[keep], :]
    return out


# version 07
# speedup vs baseline: 1.1992x; 1.1992x over previous
import numpy as np
import ml_dtypes
from contextlib import ExitStack

import concourse.bass as bass
import concourse.mybir as mybir
from concourse.bass_utils import run_bass_kernel_spmd

BFNP = ml_dtypes.bfloat16
F32 = mybir.dt.float32
BF16 = mybir.dt.bfloat16
AF = mybir.ActivationFunctionType
ALU = mybir.AluOpType
AX = mybir.AxisListType.X

B, L, D = 4, 8192, 1024
BLOCKS = 2731
KT_N = 22
KEYS_PAD = 2816
NQ = 1408
NQT = 11
HALF_LEN = [1366, 1365]
OFFS = [0, 1366]
HEADS, DK = 16, 64
NE, DH = 8, 2048
QCL = [(0, 512, 4), (512, 512, 4), (1024, 384, 3)]
N_SLOT = 3

BAND_OPS = {
    0: [(0, 0, 0, 130), (1, 128, 126, 132), (2, 256, 254, 132), (3, 384, 382, 130),
        (4, 512, 510, 2), (21, "wrap", 0, 2)],
    1: [(3, -128, 0, 2), (4, 0, 0, 130), (5, 128, 126, 132), (6, 256, 254, 132),
        (7, 384, 382, 130), (8, 512, 510, 2)],
    2: [(7, -128, 0, 2), (8, 0, 0, 130), (9, 128, 126, 132), (10, "edge", 254, 130)],
}
DELTAS = [-128, 0, 128, 256, 384, 512]
DELTA_CS = {-128: 0, 0: 0, 128: 126, 256: 254, 384: 382, 512: 510}
DELTA_W = {-128: 2, 0: 130, 128: 132, 256: 132, 384: 130, 512: 2}


def slot_meta(core, j):
    hu = 3 * core + j
    u, half = hu // 2, hu % 2
    return u // 3, u % 3, half, OFFS[half], HALF_LEN[half]


def stage_xk(x, core, j):
    b, s, half, off, hlen = slot_meta(core, j)
    out = np.zeros((KEYS_PAD, D), dtype=np.float32)
    blocks = (off + np.arange(BLOCKS)) % BLOCKS
    pos = 3 * blocks + s
    valid = pos <= L - 1
    out[:BLOCKS][valid] = x[b, pos[valid], :]
    return out


def stage_ccol(core, j):
    _, _, half, off, hlen = slot_meta(core, j)
    c = np.zeros((NQ,), dtype=np.float32)
    q = off + np.arange(hlen)
    c[:hlen] = np.maximum(0, 2 - q) + np.maximum(0, q - (BLOCKS - 3))
    return c


def delta_mask_np(delta, colstart, width):
    r = np.arange(128)[:, None]
    col = colstart + np.arange(width)[None, :]
    return np.where(np.abs(delta + r - col) <= 2, 2.0, 1.0)


def stage_edge_mask(core, j):
    _, _, half, _, _ = slot_meta(core, j)
    m = delta_mask_np(256, 254, 130)
    if half == 1:
        for (slot_, q_) in [(1365, 1363), (1365, 1364), (1366, 1364)]:
            m[slot_ - 1280, q_ - 1278] = 1.0
    return m


def stage_wrap_mask(core, j):
    _, _, half, _, _ = slot_meta(core, j)
    m = np.ones((128, 2))
    if half == 1:
        m[41, 0] = 2.0
        m[42, 0] = 2.0
        m[42, 1] = 2.0
    return m


class Plan:
    ENGINES = ("sync", "gpsimd", "scalar", "vector", "tensor")

    def __init__(self, sems):
        self.ops = {e: [] for e in self.ENGINES}
        self.cnt = {}
        self.sems = sems

    def on(self, eng):
        def deco(fn):
            self.ops[eng].append(fn)
        return deco

    def wait(self, eng, name, val):
        if val > 0:
            s = self.sems[name]
            self.ops[eng].append(lambda E, s=s, v=val: E.wait_ge(s, v))

    def inc(self, name, k=1):
        self.cnt[name] = self.cnt.get(name, 0) + k
        return self.cnt[name]

    def n(self, name):
        return self.cnt.get(name, 0)


class RotBuf:

    def __init__(self, plan, n):
        self.plan, self.nb = plan, n
        self.uses = []
        self.pending = 0

    def acquire(self, eng):
        i = len(self.uses) + self.pending
        if i >= self.nb:
            nm, v = self.uses[i - self.nb]
            self.plan.wait(eng, nm, v)
        self.pending += 1
        return i % self.nb

    def commit(self, semname, val):
        assert self.pending > 0
        self.pending -= 1
        self.uses.append((semname, val))


def build_program(n_slot=N_SLOT, heads=HEADS, do_attn=True, do_ffn=True, n_exp=NE,
                  debug_taps=False):
    nc = bass.Bass("TRN2", target_bir_lowering=False, debug=False)

    xk_d = nc.dram_tensor("xk", [N_SLOT, KEYS_PAD, D], F32, kind="ExternalInput").ap()
    wq_d = nc.dram_tensor("wq", [8, 8, 128, 128], F32, kind="ExternalInput").ap()
    wk_d = nc.dram_tensor("wk", [8, 8, 128, 128], F32, kind="ExternalInput").ap()
    wv_d = nc.dram_tensor("wv65", [8, 128, 1040], F32, kind="ExternalInput").ap()
    gw_d = nc.dram_tensor("gwt", [8, 128, 8], F32, kind="ExternalInput").ap()
    bb_d = nc.dram_tensor("bbias", [128, 8], F32, kind="ExternalInput").ap()
    w1_d = nc.dram_tensor("w1t", [8, 8, 128, DH], F32, kind="ExternalInput").ap()
    w2_d = nc.dram_tensor("w2t", [8, 16, 128, D], F32, kind="ExternalInput").ap()
    cc_d = nc.dram_tensor("ccol", [128, N_SLOT, NQT], F32, kind="ExternalInput").ap()
    me_d = nc.dram_tensor("medge", [N_SLOT, 128, 130], F32, kind="ExternalInput").ap()
    mw_d = nc.dram_tensor("mwrap", [N_SLOT, 128, 2], F32, kind="ExternalInput").ap()

    xq_s = nc.dram_tensor("xnts", [N_SLOT, 8, 128, NQ], BF16, kind="Internal").ap()
    out_d = nc.dram_tensor("out", [N_SLOT, NQ, D], F32, kind="ExternalOutput").ap()
    tap_d = {}
    if debug_taps:
        tap_d["t_xnt"] = nc.dram_tensor("t_xnt", [8, 128, KEYS_PAD], BF16, kind="ExternalOutput").ap()
        tap_d["t_kt"] = nc.dram_tensor("t_kt", [8, 128, KEYS_PAD], BF16, kind="ExternalOutput").ap()
        tap_d["t_qt"] = nc.dram_tensor("t_qt", [8, 128, NQ], BF16, kind="ExternalOutput").ap()
        tap_d["t_v65"] = nc.dram_tensor("t_v65", [KT_N, 128, 1040], BF16, kind="ExternalOutput").ap()
        tap_d["t_attn"] = nc.dram_tensor("t_attn", [NQ, D], F32, kind="ExternalOutput").ap()
        tap_d["t_comb"] = nc.dram_tensor("t_comb", [128, NQT, 8], F32, kind="ExternalOutput").ap()
        tap_d["t_stats"] = nc.dram_tensor("t_stats", [128, 16], F32, kind="ExternalOutput").ap()
        tap_d["t_xn"] = nc.dram_tensor("t_xn", [128, 2, D], BF16, kind="ExternalOutput").ap()
        tap_d["t_xt"] = nc.dram_tensor("t_xt", [128, 2, D], F32, kind="ExternalOutput").ap()

    mask_np = np.ones((128, 6, 132), dtype=BFNP)
    for i, dl in enumerate(DELTAS):
        w = DELTA_W[dl]
        mask_np[:, i, :w] = delta_mask_np(dl, DELTA_CS[dl], w).astype(BFNP)
    maskd_dram = nc.inline_tensor(np.ascontiguousarray(mask_np), name="maskd").ap()
    idf_dram = nc.inline_tensor(np.eye(128, dtype=np.float32), name="idf").ap()
    idb_dram = nc.inline_tensor(np.eye(128, dtype=BFNP), name="idb").ap()
    eps_dram = nc.inline_tensor(np.full((128, 1), 1e-5, dtype=np.float32), name="epsc").ap()

    es = ExitStack()

    def sb(name, shape, dtype):
        return es.enter_context(nc.sbuf_tensor(name, shape, dtype))

    def psumt(name, shape, dtype):
        return es.enter_context(nc.psum_tensor(name, shape, dtype))

    XnT = sb("XnT", [128, 8, KEYS_PAD], BF16)
    KT = sb("KT", [128, 8, KEYS_PAD], BF16)
    V65 = sb("V65", [128, KT_N, 1040], BF16)
    QT = sb("QT", [128, 8, NQ], BF16)
    UU = sb("UU", [128, 8320], BF16)
    x_tile = sb("x_tile", [128, 2, D], F32)
    xn_tile = sb("xn_tile", [128, 2, D], BF16)
    w_slab = sb("w_slab", [128, 2, 8, 128], BF16)
    pt_buf = sb("pt_buf", [128, 3, 1024], BF16)
    ot_stage = sb("ot_stage", [128, 2, 512], F32)
    xsq = ot_stage[:].rearrange("p a b -> p (a b)")
    stats = sb("stats", [128, 16], F32)
    ccol = sb("ccol_s", [128, N_SLOT, NQT], F32)
    maskd = sb("maskd_s", [128, 6, 132], BF16)
    medge = sb("medge_s", [128, N_SLOT, 130], BF16)
    mwrap = sb("mwrap_s", [128, N_SLOT, 2], BF16)
    identf = sb("identf", [128, 128], F32)
    identb = sb("identb", [128, 128], BF16)
    biasb = sb("biasb", [128, 8], F32)
    epsc = sb("epsc_s", [128, 1], F32)
    gwt = sb("gwt_s", [128, 8, 8], BF16)
    gbuf = sb("gbuf", [128, 6, 8], F32)
    gsml = sb("gsml", [128, 8], F32)
    comb = sb("comb", [128, N_SLOT * NQT, 8], F32)

    attn_out = UU[:].bitcast(F32)[:, 0:4096].rearrange("p (a b) -> p a b", a=4)
    wv65_sb = UU[:].rearrange("p (a b) -> p a b", a=8)
    Hsb = UU[:][:, 0:8192].rearrange("p (a b) -> p a b", a=16)
    w1A = XnT[:][:, :, 0:DH]
    w1B = KT[:][:, :, 0:DH]
    w2S = V65[:][:, 0:16, 0:D]
    xnq_sb = QT[:]
    eo_stage = x_tile

    ps_bigF = [psumt("ps_bigF0", [128, 1024], F32), psumt("ps_bigF1", [128, 1024], F32),
               psumt("ps_bigF2", [128, 1024], F32)]
    ps_mm = [b[:, c:c + 512] for b in ps_bigF for c in (0, 512)]
    ps_bigS = [b[:] for b in ps_bigF]
    ps_o = [psumt("ps_o0", [128, 512], F32), psumt("ps_o1", [128, 512], F32)]
    ps_t = [ps_o[0][:].bitcast(BF16), ps_o[1][:].bitcast(BF16)]
    Dm = sb("Dm", [128, NQT, 16], F32)
    dche = sb("dche", [128, 16], F32)
    drec = sb("drec", [128, 16], F32)

    sem_names = ["init_s", "init_g", "xload", "sq", "var", "rstd", "xn", "tpd",
                 "xnt", "wload", "kqe", "v65e", "v65s", "ones", "gmm", "gate_v",
                 "gate_s", "combr", "sS", "sP", "sB", "sOm", "sOe", "sTp", "sOn", "dme",
                 "aout", "xq", "xacc", "so", "qstore", "fenceT", "cw", "cxq", "cH", "cSi",
                 "cEOg", "cEO", "cEv", "cAcc", "tapd"]
    sems = {nm: es.enter_context(nc.semaphore(nm)) for nm in sem_names}
    P = Plan(sems)
    Se = sems

    mm_rb = RotBuf(P, 6)
    s_rb = RotBuf(P, 3)
    po_rb = RotBuf(P, 2)
    xt_rb = RotBuf(P, 2)
    xnt_rb = RotBuf(P, 2)
    ws_rb = RotBuf(P, 2)
    ot_rb = RotBuf(P, 2)
    hs_rb = RotBuf(P, 1)
    w1_rb = RotBuf(P, 2)
    w2_rb = RotBuf(P, 1)
    xq_rb = RotBuf(P, 1)

    def p_init():
        @P.on("sync")
        def _(E):
            E.dma_start(out=ccol[:], in_=cc_d[:]).then_inc(Se["init_s"], 16)
            E.dma_start(out=identf[:], in_=idf_dram[:]).then_inc(Se["init_s"], 16)
            E.dma_start(out=identb[:], in_=idb_dram[:]).then_inc(Se["init_s"], 16)
            E.dma_start(out=biasb[:], in_=bb_d[:]).then_inc(Se["init_s"], 16)
            E.dma_start(out=epsc[:], in_=eps_dram[:]).then_inc(Se["init_s"], 16)
        P.inc("init_s", 5 * 16)

        @P.on("gpsimd")
        def _(E):
            E.dma_start(out=maskd[:], in_=maskd_dram[:]).then_inc(Se["init_g"], 16)
            E.dma_start(out=medge[:], in_=me_d.rearrange("a p b -> p a b")).then_inc(Se["init_g"], 16)
            E.dma_start(out=mwrap[:], in_=mw_d.rearrange("a p b -> p a b")).then_inc(Se["init_g"], 16)
            E.dma_start(out=gwt[:], in_=gw_d.rearrange("a p b -> p a b")).then_inc(Se["init_g"], 16)
        P.inc("init_g", 4 * 16)
        for e in Plan.ENGINES:
            P.wait(e, "init_s", P.n("init_s"))
            P.wait(e, "init_g", P.n("init_g"))

    def p1(slot):
        for kt in range(KT_N):
            xp = xt_rb.acquire("sync")
            xl = P.inc("xload", 16)

            @P.on("sync")
            def _(E, sl=slot, k=kt, p=xp):
                E.dma_start(out=x_tile[:, p, :],
                            in_=xk_d[sl, k * 128:(k + 1) * 128, :]).then_inc(Se["xload"], 16)

            P.wait("scalar", "xload", xl)
            if P.n("sq") >= 2:
                P.wait("scalar", "var", P.n("sq") - 1)
            sq = P.inc("sq")

            @P.on("scalar")
            def _(E, p=xp):
                E.activation(out=xsq, in_=x_tile[:, p, :], func=AF.Square,
                             accum_out=stats[:, 8 + p:9 + p]).then_inc(Se["sq"], 1)

            P.wait("vector", "xload", xl)

            @P.on("vector")
            def _(E, p=xp):
                E.reduce_sum(stats[:, 0 + p:1 + p], x_tile[:, p, :], axis=AX)
                E.drain()
                E.tensor_scalar(out=stats[:, 2 + p:3 + p], in0=stats[:, 0 + p:1 + p],
                                scalar1=-1.0 / D, scalar2=None, op0=ALU.mult)
                E.drain()
                E.tensor_mul(stats[:, 4 + p:5 + p], stats[:, 2 + p:3 + p], stats[:, 2 + p:3 + p])
                E.drain()
            P.wait("vector", "sq", sq)
            if P.n("var") >= 2:
                P.wait("vector", "rstd", P.n("var") - 1)
            vv = P.inc("var")

            @P.on("vector")
            def _(E, p=xp):
                E.scalar_tensor_tensor(out=stats[:, 6 + p:7 + p], in0=stats[:, 8 + p:9 + p],
                                       scalar=1.0 / D, in1=stats[:, 4 + p:5 + p],
                                       op0=ALU.mult, op1=ALU.subtract).then_inc(Se["var"], 1)

            P.wait("scalar", "var", vv)
            if P.n("rstd") >= 2:
                P.wait("scalar", "xn", P.n("rstd") - 1)
            rs = P.inc("rstd")

            @P.on("scalar")
            def _(E, p=xp):
                E.activation(out=stats[:, 10 + p:11 + p], in_=stats[:, 6 + p:7 + p],
                             func=AF.Ln, bias=epsc[:])
                E.drain()
                E.activation(out=stats[:, 12 + p:13 + p], in_=stats[:, 10 + p:11 + p],
                             func=AF.Exp, scale=-0.5).then_inc(Se["rstd"], 1)

            P.wait("vector", "rstd", rs)
            np_ = xnt_rb.acquire("vector")
            xv = P.inc("xn")

            @P.on("vector")
            def _(E, p=xp, q=np_):
                E.tensor_scalar(out=xn_tile[:, q, :], in0=x_tile[:, p, :],
                                scalar1=stats[:, 2 + p:3 + p], scalar2=stats[:, 12 + p:13 + p],
                                op0=ALU.add, op1=ALU.mult).then_inc(Se["xn"], 1)
            xt_rb.commit("xn", xv)

            P.wait("tensor", "xn", xv)
            tpp = po_rb.acquire("tensor")
            tp = P.inc("tpd")

            @P.on("tensor")
            def _(E, p=np_, t=tpp):
                for dk in range(7):
                    E.transpose(ps_t[t][:, dk * 128:(dk + 1) * 128],
                                xn_tile[:, p, dk * 128:(dk + 1) * 128], identb[:])
                E.transpose(ps_t[t][:, 7 * 128:8 * 128],
                            xn_tile[:, p, 7 * 128:8 * 128], identb[:]).then_inc(Se["tpd"], 1)
            xnt_rb.commit("tpd", tp)

            P.wait("vector", "tpd", tp)
            xe = P.inc("xnt")

            @P.on("vector")
            def _(E, t=tpp, k=kt):
                E.tensor_copy(out=XnT[:, :, k * 128:(k + 1) * 128],
                              in_=ps_t[t].rearrange("p (a b) -> p a b", a=8)).then_inc(Se["xnt"], 1)
            po_rb.commit("xnt", xe)

    def p2(slot):
        xnt_done = P.n("xnt")
        P.wait("sync", "xnt", xnt_done)
        qs = P.inc("qstore", 16)

        @P.on("sync")
        def _(E, sl=slot):
            E.dma_start(out=xq_s[sl].rearrange("a p b -> p a b"),
                        in_=XnT[:, :, 0:NQ]).then_inc(Se["qstore"], 16)

        P.wait("gpsimd", "so", P.n("so"))
        wv = P.inc("wload", 16)

        @P.on("gpsimd")
        def _(E):
            E.dma_start(out=wv65_sb, in_=wv_d.rearrange("a p b -> p a b")).then_inc(Se["wload"], 16)

        P.wait("tensor", "sP", P.n("sP"))

        def qk_proj(w_dram, dst, n_total):
            for do in range(8):
                sp_ = ws_rb.acquire("gpsimd")
                wl = P.inc("wload", 16)

                @P.on("gpsimd")
                def _(E, a=w_dram[do], p=sp_):
                    E.dma_start(out=w_slab[:, p, :, :],
                                in_=a.rearrange("a p b -> p a b")).then_inc(Se["wload"], 16)
                nch = [(i * 512, min(512, n_total - i * 512))
                       for i in range((n_total + 511) // 512)]
                last_ss = 0
                for (n0, nw) in nch:
                    mp = mm_rb.acquire("tensor")
                    P.wait("tensor", "wload", wl)
                    P.wait("tensor", "xnt", xnt_done)
                    ss = P.inc("sS")
                    last_ss = ss

                    @P.on("tensor")
                    def _(E, p=sp_, q=mp, a=n0, w=nw):
                        for dk in range(7):
                            E.matmul(ps_mm[q][:, 0:w], lhsT=w_slab[:, p, dk, :],
                                     rhs=XnT[:, dk, a:a + w], start=(dk == 0), stop=False)
                        E.matmul(ps_mm[q][:, 0:w], lhsT=w_slab[:, p, 7, :],
                                 rhs=XnT[:, 7, a:a + w], start=False, stop=True).then_inc(Se["sS"], 1)
                    P.wait("scalar", "sS", ss)
                    kq = P.inc("kqe")

                    @P.on("scalar")
                    def _(E, q=mp, d=do, a=n0, w=nw, t=dst):
                        E.activation(out=t[:, d, a:a + w], in_=ps_mm[q][:, 0:w],
                                     func=AF.Copy).then_inc(Se["kqe"], 1)
                    mm_rb.commit("kqe", kq)
                ws_rb.commit("sS", last_ss)

        qk_proj(wq_d, QT[:], NQ)
        qk_proj(wk_d, KT[:], KEYS_PAD)

        P.wait("tensor", "wload", wv)
        for kt in range(KT_N):
            for ch in range(4):
                vp = po_rb.acquire("tensor")
                ve = P.inc("v65e")

                @P.on("tensor")
                def _(E, k=kt, c=ch, q=vp):
                    for dk in range(7):
                        E.matmul(ps_o[q][:, 0:260], lhsT=XnT[:, dk, k * 128:(k + 1) * 128],
                                 rhs=wv65_sb[:, dk, c * 260:(c + 1) * 260],
                                 start=(dk == 0), stop=False)
                    E.matmul(ps_o[q][:, 0:260], lhsT=XnT[:, 7, k * 128:(k + 1) * 128],
                             rhs=wv65_sb[:, 7, c * 260:(c + 1) * 260],
                             start=False, stop=True).then_inc(Se["v65e"], 1)
                P.wait("scalar", "v65e", ve)
                vs = P.inc("v65s")

                @P.on("scalar")
                def _(E, k=kt, c=ch, q=vp):
                    E.activation(out=V65[:, k, c * 260:(c + 1) * 260], in_=ps_o[q][:, 0:260],
                                 func=AF.Copy).then_inc(Se["v65s"], 1)
                po_rb.commit("v65s", vs)
        P.wait("vector", "v65s", P.n("v65s"))

        @P.on("vector")
        def _(E):
            for k in range(KT_N - 1):
                E.memset(V65[:, k, 64:1040:65], 1.0)
            E.memset(V65[0:43, KT_N - 1, 64:1040:65], 1.0).then_inc(Se["ones"], 1)
        P.inc("ones")

        for tt in range(NQT):
            mp = mm_rb.acquire("tensor")
            gm = P.inc("gmm")

            @P.on("tensor")
            def _(E, t=tt, q=mp):
                for dk in range(7):
                    E.matmul(ps_mm[q][:, 0:8], lhsT=XnT[:, dk, t * 128:(t + 1) * 128],
                             rhs=gwt[:, dk, :], start=(dk == 0), stop=False)
                E.matmul(ps_mm[q][:, 0:8], lhsT=XnT[:, 7, t * 128:(t + 1) * 128],
                         rhs=gwt[:, 7, :], start=False, stop=True).then_inc(Se["gmm"], 1)
            P.wait("vector", "gmm", gm)
            gv = P.inc("gate_v")

            @P.on("vector")
            def _(E, q=mp):
                E.tensor_tensor(gbuf[:, 0, :], ps_mm[q][:, 0:8], biasb[:],
                                op=ALU.add).then_inc(Se["gate_v"], 1)
            mm_rb.commit("gate_v", gv)
            P.wait("scalar", "gate_v", gv)
            gs = P.inc("gate_s")

            @P.on("scalar")
            def _(E):
                E.activation(out=gbuf[:, 1, :], in_=gbuf[:, 0, :],
                             func=AF.Sigmoid).then_inc(Se["gate_s"], 1)
            P.wait("vector", "gate_s", gs)
            ci = slot * NQT + tt
            cr = P.inc("combr")

            @P.on("vector")
            def _(E, c=ci):
                g2 = gbuf[:, 1, :]
                E.reduce_max(gsml[:, 0:1], g2, axis=AX)
                E.drain()
                E.tensor_scalar(out=gbuf[:, 2, :], in0=g2, scalar1=gsml[:, 0:1],
                                scalar2=None, op0=ALU.is_ge)
                E.drain()
                E.scalar_tensor_tensor(out=gbuf[:, 3, :], in0=gbuf[:, 2, :], scalar=-1e9,
                                       in1=g2, op0=ALU.mult, op1=ALU.add)
                E.drain()
                E.reduce_max(gsml[:, 1:2], gbuf[:, 3, :], axis=AX)
                E.drain()
                E.tensor_scalar(out=gbuf[:, 4, :], in0=g2, scalar1=gsml[:, 1:2],
                                scalar2=None, op0=ALU.is_ge)
                E.drain()
                E.tensor_mul(gbuf[:, 5, :], g2, gbuf[:, 4, :])
                E.tensor_add(gsml[:, 2:3], gsml[:, 0:1], gsml[:, 1:2])
                E.drain()
                E.tensor_scalar(out=gsml[:, 2:3], in0=gsml[:, 2:3], scalar1=1e-6,
                                scalar2=None, op0=ALU.add)
                E.drain()
                E.reciprocal(gsml[:, 3:4], gsml[:, 2:3])
                E.drain()
                E.tensor_scalar(out=comb[:, c, :], in0=gbuf[:, 5, :], scalar1=gsml[:, 3:4],
                                scalar2=None, op0=ALU.mult).then_inc(Se["combr"], 1)

    def p3(slot):
        v65_ready = P.n("v65s")
        ones_ready = P.n("ones")
        kq_ready = P.n("kqe")
        P.wait("tensor", "v65s", v65_ready)
        P.wait("tensor", "ones", ones_ready)
        P.wait("tensor", "kqe", kq_ready)
        P.wait("tensor", "gate_v", P.n("gate_v"))
        for qci, (q0, qw, ntl) in enumerate(QCL):
            bops = {}
            for (kt, kind, cs, wd) in BAND_OPS[qci]:
                bops.setdefault(kt, []).append((kind, cs, wd))
            deferred = []
            for hp in range(heads // 2):
                h0, h1 = 2 * hp, 2 * hp + 1
                o0 = po_rb.acquire("tensor")
                o1 = po_rb.acquire("tensor")
                pend = []

                def emit_O_pair(fin):
                    k, t, ls, lv = pend.pop(0)
                    P.wait("tensor", ls, lv)
                    last = fin and not pend
                    if last:
                        som_ = P.inc("sOm")

                    @P.on("tensor")
                    def _(E, k=k, t=t, w=qw, A=h0, Bh=h1, a=o0, b=o1, last=last):
                        E.matmul(ps_o[a][0:65, 0:w], lhsT=V65[:, k, 65 * A:65 * A + 65],
                                 rhs=pt_buf[:, t, 0:w], start=(k == 0), stop=(k == KT_N - 1))
                        mm = E.matmul(ps_o[b][0:65, 0:w], lhsT=V65[:, k, 65 * Bh:65 * Bh + 65],
                                      rhs=pt_buf[:, t, 512:512 + w],
                                      start=(k == 0), stop=(k == KT_N - 1))
                        if last:
                            mm.then_inc(Se["sOm"], 1)
                    return P.n("sOm") if last else None

                for kt in range(KT_N):
                    sq_ = s_rb.acquire("tensor")
                    ss = P.inc("sS")

                    @P.on("tensor")
                    def _(E, k=kt, q=sq_, a=q0, w=qw, HP=hp):
                        E.matmul(ps_bigS[q][:, 0:w],
                                 lhsT=KT[0:64, HP, k * 128:(k + 1) * 128],
                                 rhs=QT[0:64, HP, a:a + w],
                                 start=True, stop=True, tile_position=(0, 0))
                        E.matmul(ps_bigS[q][:, 512:512 + w],
                                 lhsT=KT[64:128, HP, k * 128:(k + 1) * 128],
                                 rhs=QT[64:128, HP, a:a + w],
                                 start=True, stop=True,
                                 tile_position=(64, 0)).then_inc(Se["sS"], 1)
                    if kt == 1 and deferred:
                        for fn in deferred:
                            fn()
                        deferred = []
                    if pend:
                        emit_O_pair(False)
                    P.wait("scalar", "sS", ss)
                    sp = P.inc("sP")
                    ptp = (sp - 1) % 3

                    @P.on("scalar")
                    def _(E, q=sq_, t=ptp, fd=512 + qw):
                        E.activation(out=pt_buf[:, t, 0:fd], in_=ps_bigS[q][:, 0:fd],
                                     func=AF.Exp, scale=0.125).then_inc(Se["sP"], 1)
                    s_rb.commit("sP", sp)
                    last_sem, last_val = "sP", sp
                    if kt in bops:
                        P.wait("vector", "sP", sp)
                        for (kind, cs, wd) in bops[kt]:
                            if kind == "edge":
                                msrc = medge[:, slot, 0:wd]
                            elif kind == "wrap":
                                msrc = mwrap[:, slot, 0:wd]
                            else:
                                msrc = maskd[:, DELTAS.index(kind), 0:wd]
                            P.inc("sB")

                            @P.on("vector")
                            def _(E, t=ptp, c=cs, w=wd, m=msrc):
                                E.tensor_mul(pt_buf[:, t, c:c + w], pt_buf[:, t, c:c + w], m)
                            P.inc("sB")

                            @P.on("vector")
                            def _(E, t=ptp, c=512 + cs, w=wd, m=msrc):
                                E.tensor_mul(pt_buf[:, t, c:c + w], pt_buf[:, t, c:c + w],
                                             m).then_inc(Se["sB"], 2)
                        last_sem, last_val = "sB", P.n("sB")
                    pend.append((kt, ptp, last_sem, last_val))
                while len(pend) > 1:
                    emit_O_pair(False)
                som = emit_O_pair(True)
                P.wait("vector", "sOm", som)
                ot0 = ot_rb.acquire("vector")
                ot1 = ot_rb.acquire("vector")
                soe = P.inc("sOe")

                @P.on("vector")
                def _(E, a=o0, b=o1, t0=ot0, t1=ot1, w=qw):
                    E.tensor_copy(out=ot_stage[:, t0, 0:w], in_=ps_o[a][:, 0:w])
                    E.tensor_copy(out=ot_stage[:, t1, 0:w],
                                  in_=ps_o[b][:, 0:w]).then_inc(Se["sOe"], 1)
                po_rb.commit("sOe", soe)
                po_rb.commit("sOe", soe)

                def make_tail(h0=h0, h1=h1, ot0=ot0, ot1=ot1, soe=soe, n=ntl, t0q=q0 // 128):
                    def tail():
                        P.wait("tensor", "sOe", soe)
                        trq = s_rb.acquire("tensor")
                        stp = P.inc("sTp")

                        @P.on("tensor")
                        def _(E, q=trq):
                            for par, ot in ((0, ot0), (1, ot1)):
                                for c in range(n):
                                    E.transpose(ps_bigS[q][:, 256 * par + c * 64:256 * par + (c + 1) * 64],
                                                ot_stage[0:64, ot, c * 128:(c + 1) * 128],
                                                identf[0:64, 0:64])
                            for par, ot in ((0, ot0), (1, ot1)):
                                for c in range(n):
                                    mm = E.transpose(
                                        ps_bigS[q][:, 512 + 8 * par + c:512 + 8 * par + c + 1],
                                        ot_stage[64:65, ot, c * 128:(c + 1) * 128],
                                        identf[64:65, 64:65])
                            mm.then_inc(Se["sTp"], 1)
                        ot_rb.commit("sTp", stp)
                        ot_rb.commit("sTp", stp)
                        P.wait("vector", "sTp", stp)
                        son = P.inc("sOn")

                        @P.on("vector")
                        def _(E, q=trq):
                            E.tensor_copy(out=attn_out[:, 0:n, 64 * h0:64 * h0 + 64],
                                          in_=ps_bigS[q][:, 0:n * 64].rearrange("p (a b) -> p a b", b=64))
                            E.tensor_copy(out=attn_out[:, 0:n, 64 * h1:64 * h1 + 64],
                                          in_=ps_bigS[q][:, 256:256 + n * 64].rearrange("p (a b) -> p a b", b=64))
                            E.tensor_copy(out=Dm[:, t0q:t0q + n, h0:h0 + 1],
                                          in_=ps_bigS[q][:, 512:512 + n].rearrange("p (a b) -> p a b", b=1))
                            E.tensor_copy(out=Dm[:, t0q:t0q + n, h1:h1 + 1],
                                          in_=ps_bigS[q][:, 520:520 + n].rearrange("p (a b) -> p a b", b=1)) \
                                .then_inc(Se["sOn"], 1)
                        s_rb.commit("sOn", son)
                    return tail
                deferred.append(make_tail())
            for fn in deferred:
                fn()
            deferred = []
            if debug_taps and slot == 0:
                P.wait("sync", "sOn", P.n("sOn"))
                for tl in range(ntl):
                    td = P.inc("tapd", 16)

                    @P.on("sync")
                    def _(E, t=q0 // 128 + tl, c=tl):
                        E.dma_start(out=tap_d["t_attn"][t * 128:(t + 1) * 128, :],
                                    in_=attn_out[:, c, :]).then_inc(Se["tapd"], 16)
                P.wait("vector", "tapd", P.n("tapd"))
            for tl in range(ntl):
                tt_g = q0 // 128 + tl
                ao = P.inc("aout")

                @P.on("vector")
                def _(E, c=tl, t=tt_g, sl=slot):
                    E.tensor_scalar(out=dche[:], in0=Dm[:, t, :],
                                    scalar1=ccol[:, sl, t:t + 1], scalar2=None, op0=ALU.add)
                    E.drain()
                    E.reciprocal(drec[:], dche[:])
                    E.drain()
                    for H in range(15):
                        E.tensor_scalar(out=attn_out[:, c, 64 * H:64 * H + 64],
                                        in0=attn_out[:, c, 64 * H:64 * H + 64],
                                        scalar1=drec[:, H:H + 1], scalar2=None, op0=ALU.mult)
                    E.tensor_scalar(out=attn_out[:, c, 960:1024],
                                    in0=attn_out[:, c, 960:1024],
                                    scalar1=drec[:, 15:16], scalar2=None,
                                    op0=ALU.mult).then_inc(Se["aout"], 1)
                P.wait("sync", "aout", ao)
                so = P.inc("so", 16)

                @P.on("sync")
                def _(E, sl=slot, t=tt_g, c=tl):
                    E.dma_start(out=out_d[sl, t * 128:(t + 1) * 128, :],
                                in_=attn_out[:, c, :]).then_inc(Se["so"], 16)
            P.wait("vector", "so", P.n("so"))
        P.wait("gpsimd", "so", P.n("so"))
        P.inc("xacc", 16)

        P.inc("xacc", 16)

        @P.on("gpsimd")
        def _(E, sl=slot):
            E.dma_start(out=out_d[sl, 0:NQ:2, :], in_=xk_d[sl, 0:NQ:2, :],
                        accum_op=ALU.add).then_inc(Se["xacc"], 16)
            E.dma_start(out=out_d[sl, 1:NQ:2, :], in_=xk_d[sl, 1:NQ:2, :],
                        accum_op=ALU.add).then_inc(Se["xacc"], 16)

    def pC():
        P.wait("gpsimd", "sOn", P.n("sOn"))
        P.wait("gpsimd", "sP", P.n("sP"))
        P.wait("tensor", "sP", P.n("sP"))
        P.wait("gpsimd", "so", P.n("so"))
        P.wait("scalar", "so", P.n("so"))
        P.wait("scalar", "combr", P.n("combr"))
        P.wait("sync", "sOn", P.n("sOn"))

        w1bufs = [w1A, w1B]

        def load_w1(e):
            wp = w1_rb.acquire("gpsimd")
            cw = P.inc("cw", 16)

            @P.on("gpsimd")
            def _(E, a=w1_d[e], p=wp):
                E.dma_start(out=w1bufs[p], in_=a.rearrange("a p b -> p a b")).then_inc(Se["cw"], 16)
            return wp, cw

        def load_w2(e):
            w2_rb.acquire("gpsimd")
            cw = P.inc("cw", 16)

            @P.on("gpsimd")
            def _(E, a=w2_d[e]):
                E.dma_start(out=w2S, in_=a.rearrange("a p b -> p a b")).then_inc(Se["cw"], 16)
            return cw

        w1p, w1c = load_w1(0)
        w2c = load_w2(0)
        for e in range(n_exp):
            if e + 1 < n_exp:
                nxt = load_w1(e + 1)
            for slot in range(n_slot):
                xqp = xq_rb.acquire("sync")
                P.wait("sync", "qstore", P.n("qstore"))
                cq = P.inc("cxq", 16)

                @P.on("sync")
                def _(E, sl=slot):
                    E.dma_start(out=xnq_sb, in_=xq_s[sl].rearrange("a p b -> p a b")) \
                        .then_inc(Se["cxq"], 16)
                last_H = 0
                last_EO = 0
                for (t0, tw, ntl) in QCL:
                    hsp = hs_rb.acquire("scalar")
                    P.wait("tensor", "cw", w1c)
                    P.wait("tensor", "cxq", cq)
                    for ht in range(16):
                        mp = mm_rb.acquire("tensor")
                        ch = P.inc("cH")
                        last_H = ch

                        @P.on("tensor")
                        def _(E, p=w1p, q=mp, H=ht, a=t0, w=tw):
                            for dk in range(7):
                                E.matmul(ps_mm[q][:, 0:w], lhsT=w1bufs[p][:, dk, H * 128:(H + 1) * 128],
                                         rhs=xnq_sb[:, dk, a:a + w], start=(dk == 0), stop=False)
                            E.matmul(ps_mm[q][:, 0:w], lhsT=w1bufs[p][:, 7, H * 128:(H + 1) * 128],
                                     rhs=xnq_sb[:, 7, a:a + w], start=False,
                                     stop=True).then_inc(Se["cH"], 1)
                        P.wait("scalar", "cH", ch)
                        cs = P.inc("cSi")

                        @P.on("scalar")
                        def _(E, q=mp, H=ht, w=tw):
                            E.activation(out=Hsb[:, H, 0:w], in_=ps_mm[q][:, 0:w],
                                         func=AF.Silu).then_inc(Se["cSi"], 1)
                        mm_rb.commit("cSi", cs)
                    P.wait("tensor", "cSi", P.n("cSi"))
                    P.wait("tensor", "cw", w2c)
                    for tl in range(ntl):
                        tt_g = t0 // 128 + tl
                        pb0 = po_rb.acquire("tensor")
                        pb1 = mm_rb.acquire("tensor")
                        banks = [ps_o[pb0][:], ps_mm[pb1]]
                        ceog = 0
                        for dh in range(2):
                            ce = P.inc("cEOg")
                            ceog = ce

                            @P.on("tensor")
                            def _(E, bk=banks[dh], c=tl, d=dh, w=512):
                                for ht in range(15):
                                    E.matmul(bk[:, 0:w], lhsT=Hsb[:, ht, c * 128:(c + 1) * 128],
                                             rhs=w2S[:, ht, d * 512:(d + 1) * 512],
                                             start=(ht == 0), stop=False)
                                E.matmul(bk[:, 0:w], lhsT=Hsb[:, 15, c * 128:(c + 1) * 128],
                                         rhs=w2S[:, 15, d * 512:(d + 1) * 512],
                                         start=False, stop=True).then_inc(Se["cEOg"], 1)
                        last_EO = ceog
                        P.wait("scalar", "cEOg", ceog)
                        ep = xt_rb.acquire("scalar")
                        cv = P.inc("cEv")

                        @P.on("scalar")
                        def _(E, b0=banks[0], b1=banks[1], p=ep, sl=slot, t=tt_g, ee=e):
                            E.activation(out=eo_stage[:, p, 0:512], in_=b0[:, 0:512],
                                         func=AF.Copy, scale=comb[:, sl * NQT + t, ee:ee + 1])
                            E.activation(out=eo_stage[:, p, 512:1024], in_=b1[:, 0:512],
                                         func=AF.Copy,
                                         scale=comb[:, sl * NQT + t, ee:ee + 1]).then_inc(Se["cEv"], 1)
                        po_rb.commit("cEv", cv)
                        mm_rb.commit("cEv", cv)
                        P.wait("gpsimd", "cEv", cv)
                        ca = P.inc("cAcc", 16)

                        @P.on("gpsimd")
                        def _(E, p=ep, sl=slot, t=tt_g):
                            E.dma_start(out=out_d[sl, t * 128:(t + 1) * 128, :],
                                        in_=eo_stage[:, p, :],
                                        accum_op=ALU.add).then_inc(Se["cAcc"], 16)
                        xt_rb.commit("cAcc", ca)
                    hs_rb.commit("cEOg", last_EO)
                xq_rb.commit("cH", last_H)
            w1_rb.commit("cH", last_H)
            w2_rb.commit("cEOg", last_EO)
            if e + 1 < n_exp:
                w1p, w1c = nxt
                w2c = load_w2(e + 1)
        P.wait("gpsimd", "cAcc", P.n("cAcc"))
        P.wait("gpsimd", "xacc", P.n("xacc"))

    def p_taps_p2():
        P.wait("sync", "xnt", P.n("xnt"))
        for _ in range(3):
            P.inc("tapd", 16)

        @P.on("sync")
        def _(E):
            E.dma_start(out=tap_d["t_stats"], in_=stats[:]).then_inc(Se["tapd"], 16)
            E.dma_start(out=tap_d["t_xn"], in_=xn_tile[:]).then_inc(Se["tapd"], 16)
            E.dma_start(out=tap_d["t_xt"], in_=x_tile[:]).then_inc(Se["tapd"], 16)
        P.wait("sync", "kqe", P.n("kqe"))
        P.wait("sync", "v65s", P.n("v65s"))
        P.wait("sync", "ones", P.n("ones"))
        P.wait("sync", "combr", P.n("combr"))
        for _ in range(5):
            P.inc("tapd", 16)

        @P.on("sync")
        def _(E):
            E.dma_start(out=tap_d["t_xnt"].rearrange("a p b -> p a b"),
                        in_=XnT[:]).then_inc(Se["tapd"], 16)
            E.dma_start(out=tap_d["t_kt"].rearrange("a p b -> p a b"),
                        in_=KT[:]).then_inc(Se["tapd"], 16)
            E.dma_start(out=tap_d["t_qt"].rearrange("a p b -> p a b"),
                        in_=QT[:]).then_inc(Se["tapd"], 16)
            E.dma_start(out=tap_d["t_v65"].rearrange("a p b -> p a b"),
                        in_=V65[:]).then_inc(Se["tapd"], 16)
            E.dma_start(out=tap_d["t_comb"], in_=comb[:, 0:NQT, :]).then_inc(Se["tapd"], 16)

    p_init()
    for slot in range(n_slot):
        p1(slot)
        p2(slot)
        if debug_taps and slot == 0:
            p_taps_p2()
        if do_attn:
            p3(slot)
    if do_ffn:
        pC()
    else:
        P.wait("sync", "so", P.n("so"))
    if debug_taps:
        P.wait("sync", "tapd", P.n("tapd"))

    blk = es.enter_context(nc.Block())
    for eng in Plan.ENGINES:
        def body(E, ops=tuple(P.ops[eng])):
            for fn in ops:
                fn(E)
        getattr(blk, eng)(body)
    es.close()
    return nc


def stage_core_inputs(core, x, W_q, W_k, W_v, gate_w, balance_bias, w1, w2, shared):
    xk = np.stack([stage_xk(x, core, j) for j in range(N_SLOT)])
    cc = np.stack([stage_ccol(core, j) for j in range(N_SLOT)])
    ccol = np.ascontiguousarray(cc.reshape(N_SLOT, NQT, 128).transpose(2, 0, 1)).astype(np.float32)
    medge = np.stack([stage_edge_mask(core, j) for j in range(N_SLOT)]).astype(np.float32)
    mwrap = np.stack([stage_wrap_mask(core, j) for j in range(N_SLOT)]).astype(np.float32)
    m = dict(shared)
    m.update(xk=np.ascontiguousarray(xk), ccol=ccol,
             medge=np.ascontiguousarray(medge), mwrap=np.ascontiguousarray(mwrap))
    return m


def make_shared(W_q, W_k, W_v, gate_w, balance_bias, w1, w2):
    wq = np.ascontiguousarray(W_q.reshape(8, 128, 8, 128).transpose(2, 0, 1, 3))
    wk = np.ascontiguousarray(W_k.reshape(8, 128, 8, 128).transpose(2, 0, 1, 3))
    wv65 = np.zeros((1024, 1040), dtype=np.float32)
    for h in range(HEADS):
        wv65[:, 65 * h:65 * h + 64] = W_v[:, 64 * h:64 * h + 64]
    wv65 = np.ascontiguousarray(wv65.reshape(8, 128, 1040))
    gwt = np.ascontiguousarray(gate_w.T.reshape(8, 128, 8))
    bb = np.ascontiguousarray(np.tile(balance_bias.reshape(1, 8), (128, 1))).astype(np.float32)
    w1t = np.ascontiguousarray(w1.transpose(0, 2, 1).reshape(8, 8, 128, DH))
    w2t = np.ascontiguousarray(w2.transpose(0, 2, 1).reshape(8, 16, 128, D))
    return dict(wq=wq, wk=wk, wv65=wv65, gwt=gwt, bbias=bb, w1t=w1t, w2t=w2t)


def kernel(L0_attn_x, L0_attn_W_q, L0_attn_W_k, L0_attn_W_v,
           L0_ffn_gate_w, L0_ffn_balance_bias, L0_ffn_w1, L0_ffn_w2):
    x = np.asarray(L0_attn_x, dtype=np.float32)
    args = [np.asarray(a, dtype=np.float32) for a in
            (L0_attn_W_q, L0_attn_W_k, L0_attn_W_v, L0_ffn_gate_w,
             L0_ffn_balance_bias, L0_ffn_w1, L0_ffn_w2)]
    shared = make_shared(*args)
    nc = build_program()
    in_maps = [stage_core_inputs(c, x, *args, shared) for c in range(8)]
    res = run_bass_kernel_spmd(nc, in_maps, core_ids=list(range(8)))
    out = np.zeros((B, L, D), dtype=np.float32)
    for c in range(8):
        oc = res.results[c]["out"]
        for j in range(N_SLOT):
            b, s, half, off, hlen = slot_meta(c, j)
            i = np.arange(hlen)
            pos = 3 * (off + i) + s
            keep = pos <= L - 1
            out[b, pos[keep], :] = oc[j, i[keep], :]
    return out


# version 08
# speedup vs baseline: 1.3920x; 1.1607x over previous
import numpy as np
import ml_dtypes
from contextlib import ExitStack

import concourse.bass as bass
import concourse.mybir as mybir
from concourse.bass_utils import run_bass_kernel_spmd
from concourse.bass_core_rust import IndirectOffsetOnAxis

BFNP = ml_dtypes.bfloat16
F32 = mybir.dt.float32
BF16 = mybir.dt.bfloat16
AF = mybir.ActivationFunctionType
ALU = mybir.AluOpType
AX = mybir.AxisListType.X

B, L, D = 4, 8192, 1024
BLOCKS = 2731
KT_N = 22
KEYS_PAD = 2816
NQ = 1408
NQT = 11
HALF_LEN = [1366, 1365]
OFFS = [0, 1366]
HEADS, DK = 16, 64
NE, DH = 8, 2048
QCL = [(0, 512, 4), (512, 512, 4), (1024, 384, 3)]
N_SLOT = 3

BAND_OPS = {
    0: [(0, 0, 0, 130), (1, 128, 126, 132), (2, 256, 254, 132), (3, 384, 382, 130),
        (4, 512, 510, 2), (21, "wrap", 0, 2)],
    1: [(3, -128, 0, 2), (4, 0, 0, 130), (5, 128, 126, 132), (6, 256, 254, 132),
        (7, 384, 382, 130), (8, 512, 510, 2)],
    2: [(7, -128, 0, 2), (8, 0, 0, 130), (9, 128, 126, 132), (10, "edge", 254, 130)],
}
DELTAS = [-128, 0, 128, 256, 384, 512]
DELTA_CS = {-128: 0, 0: 0, 128: 126, 256: 254, 384: 382, 512: 510}
DELTA_W = {-128: 2, 0: 130, 128: 132, 256: 132, 384: 130, 512: 2}


def slot_meta(core, j):
    hu = 3 * core + j
    u, half = hu // 2, hu % 2
    return u // 3, u % 3, half, OFFS[half], HALF_LEN[half]


def stage_xk(x, core, j):
    b, s, half, off, hlen = slot_meta(core, j)
    out = np.zeros((KEYS_PAD, D), dtype=np.float32)
    blocks = (off + np.arange(BLOCKS)) % BLOCKS
    pos = 3 * blocks + s
    valid = pos <= L - 1
    out[:BLOCKS][valid] = x[b, pos[valid], :]
    return out


def stage_ccol(core, j):
    _, _, half, off, hlen = slot_meta(core, j)
    c = np.zeros((NQ,), dtype=np.float32)
    q = off + np.arange(hlen)
    c[:hlen] = np.maximum(0, 2 - q) + np.maximum(0, q - (BLOCKS - 3))
    return c


def delta_mask_np(delta, colstart, width):
    r = np.arange(128)[:, None]
    col = colstart + np.arange(width)[None, :]
    return np.where(np.abs(delta + r - col) <= 2, 2.0, 1.0)


def stage_edge_mask(core, j):
    _, _, half, _, _ = slot_meta(core, j)
    m = delta_mask_np(256, 254, 130)
    if half == 1:
        for (slot_, q_) in [(1365, 1363), (1365, 1364), (1366, 1364)]:
            m[slot_ - 1280, q_ - 1278] = 1.0
    return m


def stage_wrap_mask(core, j):
    _, _, half, _, _ = slot_meta(core, j)
    m = np.ones((128, 2))
    if half == 1:
        m[41, 0] = 2.0
        m[42, 0] = 2.0
        m[42, 1] = 2.0
    return m


class Plan:
    ENGINES = ("sync", "gpsimd", "scalar", "vector", "tensor")

    def __init__(self, sems):
        self.ops = {e: [] for e in self.ENGINES}
        self.cnt = {}
        self.sems = sems

    def on(self, eng):
        def deco(fn):
            self.ops[eng].append(fn)
        return deco

    def wait(self, eng, name, val):
        if val > 0:
            s = self.sems[name]
            self.ops[eng].append(lambda E, s=s, v=val: E.wait_ge(s, v))

    def inc(self, name, k=1):
        self.cnt[name] = self.cnt.get(name, 0) + k
        return self.cnt[name]

    def n(self, name):
        return self.cnt.get(name, 0)


class RotBuf:

    def __init__(self, plan, n):
        self.plan, self.nb = plan, n
        self.uses = []
        self.pending = 0

    def acquire(self, eng):
        i = len(self.uses) + self.pending
        if i >= self.nb:
            nm, v = self.uses[i - self.nb]
            self.plan.wait(eng, nm, v)
        self.pending += 1
        return i % self.nb

    def commit(self, semname, val):
        assert self.pending > 0
        self.pending -= 1
        self.uses.append((semname, val))


def build_program(n_slot=N_SLOT, heads=HEADS, do_attn=True, do_ffn=True, n_exp=NE,
                  debug_taps=False):
    nc = bass.Bass("TRN2", target_bir_lowering=False, debug=False)

    xk_d = nc.dram_tensor("xk", [N_SLOT, KEYS_PAD, D], F32, kind="ExternalInput").ap()
    wq_d = nc.dram_tensor("wq", [8, 8, 128, 128], F32, kind="ExternalInput").ap()
    wk_d = nc.dram_tensor("wk", [8, 8, 128, 128], F32, kind="ExternalInput").ap()
    wv_d = nc.dram_tensor("wv65", [8, 128, 1040], F32, kind="ExternalInput").ap()
    gw_d = nc.dram_tensor("gwt", [8, 128, 8], F32, kind="ExternalInput").ap()
    bb_d = nc.dram_tensor("bbias", [128, 8], F32, kind="ExternalInput").ap()
    w1_d = nc.dram_tensor("w1t", [8, 8, 128, DH], F32, kind="ExternalInput").ap()
    w2_d = nc.dram_tensor("w2t", [8, 16, 128, D], F32, kind="ExternalInput").ap()
    cc_d = nc.dram_tensor("ccol", [128, N_SLOT, NQT], F32, kind="ExternalInput").ap()
    me_d = nc.dram_tensor("medge", [N_SLOT, 128, 130], F32, kind="ExternalInput").ap()
    mw_d = nc.dram_tensor("mwrap", [N_SLOT, 128, 2], F32, kind="ExternalInput").ap()

    xq_s = nc.dram_tensor("xnts", [N_SLOT, 8, 128, NQ], BF16, kind="Internal").ap()
    xnrow_s = nc.dram_tensor("xnrow", [N_SLOT * NQ, D], BF16, kind="Internal").ap()
    xg_s = nc.dram_tensor("xg", [8 * 2048, D], BF16, kind="Internal").ap()
    eo_s = nc.dram_tensor("eo", [8 * 2048, D], F32, kind="Internal").ap()
    out_d = nc.dram_tensor("out", [N_SLOT, NQ, D], F32, kind="ExternalOutput").ap()
    out_flat = out_d.rearrange("s n d -> (s n) d")
    tap_d = {}
    if debug_taps is True:
        tap_d["t_xnt"] = nc.dram_tensor("t_xnt", [8, 128, KEYS_PAD], BF16, kind="ExternalOutput").ap()
        tap_d["t_kt"] = nc.dram_tensor("t_kt", [8, 128, KEYS_PAD], BF16, kind="ExternalOutput").ap()
        tap_d["t_qt"] = nc.dram_tensor("t_qt", [8, 128, NQ], BF16, kind="ExternalOutput").ap()
        tap_d["t_v65"] = nc.dram_tensor("t_v65", [KT_N, 128, 1040], BF16, kind="ExternalOutput").ap()
        tap_d["t_attn"] = nc.dram_tensor("t_attn", [NQ, D], F32, kind="ExternalOutput").ap()
        tap_d["t_comb"] = nc.dram_tensor("t_comb", [128, NQT, 8], F32, kind="ExternalOutput").ap()
        tap_d["t_stats"] = nc.dram_tensor("t_stats", [128, 16], F32, kind="ExternalOutput").ap()
        tap_d["t_xn"] = nc.dram_tensor("t_xn", [128, 2, D], BF16, kind="ExternalOutput").ap()
        tap_d["t_xt"] = nc.dram_tensor("t_xt", [128, 2, D], F32, kind="ExternalOutput").ap()
    if debug_taps:
        tap_d["t_didx"] = nc.dram_tensor("t_didx", [128, N_SLOT * NQT, 2], mybir.dt.int32,
                                         kind="ExternalOutput").ap()
        tap_d["t_cbuf"] = nc.dram_tensor("t_cbuf", [128, N_SLOT * NQT, 2], F32,
                                         kind="ExternalOutput").ap()
        tap_d["t_ind"] = nc.dram_tensor("t_ind", [128, N_SLOT * NQT, 8], F32,
                                        kind="ExternalOutput").ap()
        tap_d["t_rti"] = nc.dram_tensor("t_rti", [8, NQ], F32, kind="ExternalOutput").ap()
        tap_d["t_rtp"] = nc.dram_tensor("t_rtp", [8, NQ], F32, kind="ExternalOutput").ap()
        tap_d["t_sc"] = nc.dram_tensor("t_sc", [8, 2, 352], F32, kind="ExternalOutput").ap()
        tap_d["t_rtq"] = nc.dram_tensor("t_rtq", [8, NQ], F32, kind="ExternalOutput").ap()
        tap_d["t_xg"] = nc.dram_tensor("t_xg", [8 * 2048, D], BF16, kind="ExternalOutput").ap()
        tap_d["t_eo"] = nc.dram_tensor("t_eo", [8 * 2048, D], F32, kind="ExternalOutput").ap()

    mask_np = np.ones((128, 6, 132), dtype=BFNP)
    for i, dl in enumerate(DELTAS):
        w = DELTA_W[dl]
        mask_np[:, i, :w] = delta_mask_np(dl, DELTA_CS[dl], w).astype(BFNP)
    maskd_dram = nc.inline_tensor(np.ascontiguousarray(mask_np), name="maskd").ap()
    idf_dram = nc.inline_tensor(np.eye(128, dtype=np.float32), name="idf").ap()
    idb_dram = nc.inline_tensor(np.eye(128, dtype=BFNP), name="idb").ap()
    eps_dram = nc.inline_tensor(np.full((128, 1), 1e-5, dtype=np.float32), name="epsc").ap()
    ebm_dram = nc.inline_tensor(
        (2048.0 * np.arange(8, dtype=np.float32) - 1.0).reshape(8, 1), name="ebm").ap()

    es = ExitStack()

    def sb(name, shape, dtype):
        return es.enter_context(nc.sbuf_tensor(name, shape, dtype))

    def psumt(name, shape, dtype):
        return es.enter_context(nc.psum_tensor(name, shape, dtype))

    XnT = sb("XnT", [128, 8, KEYS_PAD], BF16)
    KT = sb("KT", [128, 8, KEYS_PAD], BF16)
    V65 = sb("V65", [128, KT_N, 1040], BF16)
    QT = sb("QT", [128, 8, NQ], BF16)
    UU = sb("UU", [128, 8320], BF16)
    x_tile = sb("x_tile", [128, 2, D], F32)
    xn_tile = sb("xn_tile", [128, 2, D], BF16)
    w_slab = sb("w_slab", [128, 2, 8, 128], BF16)
    pt_buf = sb("pt_buf", [128, 3, 1024], BF16)
    ot_stage = sb("ot_stage", [128, 2, 512], F32)
    xsq = ot_stage[:].rearrange("p a b -> p (a b)")
    stats = sb("stats", [128, 16], F32)
    ccol = sb("ccol_s", [128, N_SLOT, NQT], F32)
    maskd = sb("maskd_s", [128, 6, 132], BF16)
    medge = sb("medge_s", [128, N_SLOT, 130], BF16)
    mwrap = sb("mwrap_s", [128, N_SLOT, 2], BF16)
    identf = sb("identf", [128, 128], F32)
    identb = sb("identb", [128, 128], BF16)
    biasb = sb("biasb", [128, 8], F32)
    epsc = sb("epsc_s", [128, 1], F32)
    gwt = sb("gwt_s", [128, 8, 8], BF16)
    gbuf = sb("gbuf", [128, 6, 8], F32)
    gsml = sb("gsml", [128, 16], F32)
    comb = sb("comb", [128, N_SLOT * NQT, 8], F32)

    attn_out = UU[:].bitcast(F32)[:, 0:4096].rearrange("p (a b) -> p a b", a=4)
    wv65_sb = UU[:].rearrange("p (a b) -> p a b", a=8)
    Hsb = UU[:][:, 0:8192].rearrange("p (a b) -> p a b", a=16)
    w1A = XnT[:][:, :, 0:DH]
    w1B = KT[:][:, :, 0:DH]
    w2S = V65[:][:, 0:16, 0:D]
    xnq_sb = QT[:]
    eo_stage = x_tile

    ps_bigF = [psumt("ps_bigF0", [128, 1024], F32), psumt("ps_bigF1", [128, 1024], F32),
               psumt("ps_bigF2", [128, 1024], F32)]
    ps_mm = [b[:, c:c + 512] for b in ps_bigF for c in (0, 512)]
    ps_bigS = [b[:] for b in ps_bigF]
    ps_o = [psumt("ps_o0", [128, 512], F32), psumt("ps_o1", [128, 512], F32)]
    ps_t = [ps_o[0][:].bitcast(BF16), ps_o[1][:].bitcast(BF16)]
    Dm = sb("Dm", [128, NQT, 16], F32)
    dche = sb("dche", [128, 16], F32)
    drec = sb("drec", [128, 16], F32)
    ind_all = sb("ind_all", [128, N_SLOT * NQT, 8], F32)
    didx = sb("didx", [128, N_SLOT * NQT, 2], mybir.dt.int32)
    cbuf = sb("cbuf", [128, N_SLOT * NQT, 2], F32)
    ebs = sb("ebs", [8, 1], F32)
    rt_f = XnT[:].rearrange("p a b -> p (a b)").bitcast(F32)
    rt_indT = rt_f[0:8, 0:NQ]
    rt_pref = rt_f[0:8, NQ:2 * NQ]
    rt_temp = rt_f[0:8, 2 * NQ:3 * NQ]
    rt_dst = rt_f[0:8, 3 * NQ:4 * NQ]
    rt_sc = sb("rt_sc", [8, 2, 352], F32)
    xrow_v = x_tile[:].bitcast(BF16)[:, :, 0:D]
    qt_flat = QT[:].rearrange("p a b -> p (a b)")
    cstage = qt_flat[:, 0:4096].rearrange("p (a b) -> p a b", a=4)
    XgT = qt_flat[:, 4096:8192].rearrange("p (a b) -> p a b", a=8)
    uu_f = UU[:].bitcast(F32)
    gA0 = uu_f[:, 0:2048].rearrange("p (a b) -> p a b", a=2)
    tmp0 = uu_f[:, 2048:3072]
    gA1 = x_tile[:]
    tmp1 = ot_stage[:].rearrange("p a b -> p (a b)")

    sem_names = ["init_s", "init_g", "xload", "sq", "var", "rstd", "xn", "tpd",
                 "xnt", "wload", "kqe", "v65e", "v65s", "ones", "gmm", "gate_v",
                 "gate_s", "combr", "sS", "sP", "sB", "sOm", "sOe", "sTp", "sOn", "dme",
                 "aout", "xq", "xacc", "so", "qstore", "fenceT", "cw", "cxq", "cH", "cSi",
                 "cEOg", "cEO", "cEv", "cAcc", "tapd", "xrow", "rt_t", "rt_v", "xsc", "cTr", "ceo", "cg", "cSum", "cXg"]
    sems = {nm: es.enter_context(nc.semaphore(nm)) for nm in sem_names}
    P = Plan(sems)
    Se = sems

    mm_rb = RotBuf(P, 6)
    s_rb = RotBuf(P, 3)
    po_rb = RotBuf(P, 2)
    xt_rb = RotBuf(P, 2)
    xnt_rb = RotBuf(P, 2)
    ws_rb = RotBuf(P, 2)
    ot_rb = RotBuf(P, 2)
    hs_rb = RotBuf(P, 1)
    w1_rb = RotBuf(P, 2)
    w2_rb = RotBuf(P, 1)
    xq_rb = RotBuf(P, 1)

    def p_init():
        @P.on("sync")
        def _(E):
            E.dma_start(out=ccol[:], in_=cc_d[:]).then_inc(Se["init_s"], 16)
            E.dma_start(out=identf[:], in_=idf_dram[:]).then_inc(Se["init_s"], 16)
            E.dma_start(out=identb[:], in_=idb_dram[:]).then_inc(Se["init_s"], 16)
            E.dma_start(out=biasb[:], in_=bb_d[:]).then_inc(Se["init_s"], 16)
            E.dma_start(out=epsc[:], in_=eps_dram[:]).then_inc(Se["init_s"], 16)
            E.dma_start(out=ebs[:], in_=ebm_dram[:]).then_inc(Se["init_s"], 16)
        P.inc("init_s", 6 * 16)

        @P.on("gpsimd")
        def _(E):
            E.dma_start(out=maskd[:], in_=maskd_dram[:]).then_inc(Se["init_g"], 16)
            E.dma_start(out=medge[:], in_=me_d.rearrange("a p b -> p a b")).then_inc(Se["init_g"], 16)
            E.dma_start(out=mwrap[:], in_=mw_d.rearrange("a p b -> p a b")).then_inc(Se["init_g"], 16)
            E.dma_start(out=gwt[:], in_=gw_d.rearrange("a p b -> p a b")).then_inc(Se["init_g"], 16)
        P.inc("init_g", 4 * 16)
        for e in Plan.ENGINES:
            P.wait(e, "init_s", P.n("init_s"))
            P.wait(e, "init_g", P.n("init_g"))

    def p1(slot):
        for kt in range(KT_N):
            xp = xt_rb.acquire("sync")
            xl = P.inc("xload", 16)

            @P.on("sync")
            def _(E, sl=slot, k=kt, p=xp):
                E.dma_start(out=x_tile[:, p, :],
                            in_=xk_d[sl, k * 128:(k + 1) * 128, :]).then_inc(Se["xload"], 16)

            P.wait("scalar", "xload", xl)
            if P.n("sq") >= 2:
                P.wait("scalar", "var", P.n("sq") - 1)
            sq = P.inc("sq")

            @P.on("scalar")
            def _(E, p=xp):
                E.activation(out=xsq, in_=x_tile[:, p, :], func=AF.Square,
                             accum_out=stats[:, 8 + p:9 + p]).then_inc(Se["sq"], 1)

            P.wait("vector", "xload", xl)

            @P.on("vector")
            def _(E, p=xp):
                E.reduce_sum(stats[:, 0 + p:1 + p], x_tile[:, p, :], axis=AX)
                E.drain()
                E.tensor_scalar(out=stats[:, 2 + p:3 + p], in0=stats[:, 0 + p:1 + p],
                                scalar1=-1.0 / D, scalar2=None, op0=ALU.mult)
                E.drain()
                E.tensor_mul(stats[:, 4 + p:5 + p], stats[:, 2 + p:3 + p], stats[:, 2 + p:3 + p])
                E.drain()
            P.wait("vector", "sq", sq)
            if P.n("var") >= 2:
                P.wait("vector", "rstd", P.n("var") - 1)
            vv = P.inc("var")

            @P.on("vector")
            def _(E, p=xp):
                E.scalar_tensor_tensor(out=stats[:, 6 + p:7 + p], in0=stats[:, 8 + p:9 + p],
                                       scalar=1.0 / D, in1=stats[:, 4 + p:5 + p],
                                       op0=ALU.mult, op1=ALU.subtract).then_inc(Se["var"], 1)

            P.wait("scalar", "var", vv)
            if P.n("rstd") >= 2:
                P.wait("scalar", "xn", P.n("rstd") - 1)
            rs = P.inc("rstd")

            @P.on("scalar")
            def _(E, p=xp):
                E.activation(out=stats[:, 10 + p:11 + p], in_=stats[:, 6 + p:7 + p],
                             func=AF.Ln, bias=epsc[:])
                E.drain()
                E.activation(out=stats[:, 12 + p:13 + p], in_=stats[:, 10 + p:11 + p],
                             func=AF.Exp, scale=-0.5).then_inc(Se["rstd"], 1)

            P.wait("vector", "rstd", rs)
            np_ = xnt_rb.acquire("vector")
            xv = P.inc("xn")

            @P.on("vector")
            def _(E, p=xp, q=np_):
                E.tensor_scalar(out=xn_tile[:, q, :], in0=x_tile[:, p, :],
                                scalar1=stats[:, 2 + p:3 + p], scalar2=stats[:, 12 + p:13 + p],
                                op0=ALU.add, op1=ALU.mult).then_inc(Se["xn"], 1)
            xt_rb.commit("xn", xv)

            P.wait("tensor", "xn", xv)
            tpp = po_rb.acquire("tensor")
            tp = P.inc("tpd")

            @P.on("tensor")
            def _(E, p=np_, t=tpp):
                for dk in range(7):
                    E.transpose(ps_t[t][:, dk * 128:(dk + 1) * 128],
                                xn_tile[:, p, dk * 128:(dk + 1) * 128], identb[:])
                E.transpose(ps_t[t][:, 7 * 128:8 * 128],
                            xn_tile[:, p, 7 * 128:8 * 128], identb[:]).then_inc(Se["tpd"], 1)
            if kt < NQT:
                P.wait("sync", "tpd", tp)
                xr = P.inc("xrow", 16)

                @P.on("sync")
                def _(E, p=np_, k=kt, sl=slot):
                    E.dma_start(out=xnrow_s[sl * NQ + k * 128:sl * NQ + (k + 1) * 128, :],
                                in_=xn_tile[:, p, :]).then_inc(Se["xrow"], 16)
                xnt_rb.commit("xrow", xr)
            else:
                xnt_rb.commit("tpd", tp)

            P.wait("vector", "tpd", tp)
            xe = P.inc("xnt")

            @P.on("vector")
            def _(E, t=tpp, k=kt):
                E.tensor_copy(out=XnT[:, :, k * 128:(k + 1) * 128],
                              in_=ps_t[t].rearrange("p (a b) -> p a b", a=8)).then_inc(Se["xnt"], 1)
            po_rb.commit("xnt", xe)

    def p2(slot):
        xnt_done = P.n("xnt")
        P.wait("sync", "xnt", xnt_done)
        qs = P.inc("qstore", 16)

        @P.on("sync")
        def _(E, sl=slot):
            E.dma_start(out=xq_s[sl].rearrange("a p b -> p a b"),
                        in_=XnT[:, :, 0:NQ]).then_inc(Se["qstore"], 16)

        P.wait("gpsimd", "so", P.n("so"))
        wv = P.inc("wload", 16)

        @P.on("gpsimd")
        def _(E):
            E.dma_start(out=wv65_sb, in_=wv_d.rearrange("a p b -> p a b")).then_inc(Se["wload"], 16)

        P.wait("tensor", "sP", P.n("sP"))

        def qk_proj(w_dram, dst, n_total):
            for do in range(8):
                sp_ = ws_rb.acquire("gpsimd")
                wl = P.inc("wload", 16)

                @P.on("gpsimd")
                def _(E, a=w_dram[do], p=sp_):
                    E.dma_start(out=w_slab[:, p, :, :],
                                in_=a.rearrange("a p b -> p a b")).then_inc(Se["wload"], 16)
                nch = [(i * 512, min(512, n_total - i * 512))
                       for i in range((n_total + 511) // 512)]
                last_ss = 0
                for (n0, nw) in nch:
                    mp = mm_rb.acquire("tensor")
                    P.wait("tensor", "wload", wl)
                    P.wait("tensor", "xnt", xnt_done)
                    ss = P.inc("sS")
                    last_ss = ss

                    @P.on("tensor")
                    def _(E, p=sp_, q=mp, a=n0, w=nw):
                        for dk in range(7):
                            E.matmul(ps_mm[q][:, 0:w], lhsT=w_slab[:, p, dk, :],
                                     rhs=XnT[:, dk, a:a + w], start=(dk == 0), stop=False)
                        E.matmul(ps_mm[q][:, 0:w], lhsT=w_slab[:, p, 7, :],
                                 rhs=XnT[:, 7, a:a + w], start=False, stop=True).then_inc(Se["sS"], 1)
                    P.wait("scalar", "sS", ss)
                    kq = P.inc("kqe")

                    @P.on("scalar")
                    def _(E, q=mp, d=do, a=n0, w=nw, t=dst):
                        E.activation(out=t[:, d, a:a + w], in_=ps_mm[q][:, 0:w],
                                     func=AF.Copy).then_inc(Se["kqe"], 1)
                    mm_rb.commit("kqe", kq)
                ws_rb.commit("sS", last_ss)

        qk_proj(wq_d, QT[:], NQ)
        qk_proj(wk_d, KT[:], KEYS_PAD)

        P.wait("tensor", "wload", wv)
        for kt in range(KT_N):
            for ch in range(4):
                vp = po_rb.acquire("tensor")
                ve = P.inc("v65e")

                @P.on("tensor")
                def _(E, k=kt, c=ch, q=vp):
                    for dk in range(7):
                        E.matmul(ps_o[q][:, 0:260], lhsT=XnT[:, dk, k * 128:(k + 1) * 128],
                                 rhs=wv65_sb[:, dk, c * 260:(c + 1) * 260],
                                 start=(dk == 0), stop=False)
                    E.matmul(ps_o[q][:, 0:260], lhsT=XnT[:, 7, k * 128:(k + 1) * 128],
                             rhs=wv65_sb[:, 7, c * 260:(c + 1) * 260],
                             start=False, stop=True).then_inc(Se["v65e"], 1)
                P.wait("scalar", "v65e", ve)
                vs = P.inc("v65s")

                @P.on("scalar")
                def _(E, k=kt, c=ch, q=vp):
                    E.activation(out=V65[:, k, c * 260:(c + 1) * 260], in_=ps_o[q][:, 0:260],
                                 func=AF.Copy).then_inc(Se["v65s"], 1)
                po_rb.commit("v65s", vs)
        P.wait("vector", "v65s", P.n("v65s"))

        @P.on("vector")
        def _(E):
            for k in range(KT_N - 1):
                E.memset(V65[:, k, 64:1040:65], 1.0)
            E.memset(V65[0:43, KT_N - 1, 64:1040:65], 1.0).then_inc(Se["ones"], 1)
        P.inc("ones")

        for tt in range(NQT):
            mp = mm_rb.acquire("tensor")
            gm = P.inc("gmm")

            @P.on("tensor")
            def _(E, t=tt, q=mp):
                for dk in range(7):
                    E.matmul(ps_mm[q][:, 0:8], lhsT=XnT[:, dk, t * 128:(t + 1) * 128],
                             rhs=gwt[:, dk, :], start=(dk == 0), stop=False)
                E.matmul(ps_mm[q][:, 0:8], lhsT=XnT[:, 7, t * 128:(t + 1) * 128],
                         rhs=gwt[:, 7, :], start=False, stop=True).then_inc(Se["gmm"], 1)
            P.wait("vector", "gmm", gm)
            gv = P.inc("gate_v")

            @P.on("vector")
            def _(E, q=mp):
                E.tensor_tensor(gbuf[:, 0, :], ps_mm[q][:, 0:8], biasb[:],
                                op=ALU.add).then_inc(Se["gate_v"], 1)
            mm_rb.commit("gate_v", gv)
            P.wait("scalar", "gate_v", gv)
            gs = P.inc("gate_s")

            @P.on("scalar")
            def _(E):
                E.activation(out=gbuf[:, 1, :], in_=gbuf[:, 0, :],
                             func=AF.Sigmoid).then_inc(Se["gate_s"], 1)
            P.wait("vector", "gate_s", gs)
            ci = slot * NQT + tt
            cr = P.inc("combr")

            @P.on("vector")
            def _(E, c=ci):
                g2 = gbuf[:, 1, :]
                E.reduce_max(gsml[:, 0:1], g2, axis=AX)
                E.drain()
                E.tensor_scalar(out=gbuf[:, 2, :], in0=g2, scalar1=gsml[:, 0:1],
                                scalar2=None, op0=ALU.is_ge)
                E.drain()
                E.scalar_tensor_tensor(out=gbuf[:, 3, :], in0=gbuf[:, 2, :], scalar=-1e9,
                                       in1=g2, op0=ALU.mult, op1=ALU.add)
                E.drain()
                E.reduce_max(gsml[:, 1:2], gbuf[:, 3, :], axis=AX)
                E.drain()
                E.tensor_scalar(out=gbuf[:, 4, :], in0=g2, scalar1=gsml[:, 1:2],
                                scalar2=None, op0=ALU.is_ge)
                E.drain()
                E.tensor_mul(gbuf[:, 5, :], g2, gbuf[:, 4, :])
                E.tensor_add(gsml[:, 2:3], gsml[:, 0:1], gsml[:, 1:2])
                E.drain()
                E.tensor_scalar(out=gsml[:, 2:3], in0=gsml[:, 2:3], scalar1=1e-6,
                                scalar2=None, op0=ALU.add)
                E.drain()
                E.reciprocal(gsml[:, 3:4], gsml[:, 2:3])
                E.drain()
                E.tensor_copy(out=ind_all[:, c, :], in_=gbuf[:, 4, :])
                E.tensor_scalar(out=comb[:, c, :], in0=gbuf[:, 5, :], scalar1=gsml[:, 3:4],
                                scalar2=None, op0=ALU.mult).then_inc(Se["combr"], 1)

    def routing(slot):
        P.wait("vector", "combr", P.n("combr"))
        P.wait("vector", "v65s", P.n("v65s"))
        P.wait("vector", "kqe", P.n("kqe"))
        for tt in range(NQT):
            ci = slot * NQT + tt
            trq = s_rb.acquire("tensor")
            rtt = P.inc("rt_t")

            @P.on("tensor")
            def _(E, q=trq, c=ci):
                E.transpose(ps_bigS[q][0:8, 0:128], ind_all[:, c, :],
                            identf[:]).then_inc(Se["rt_t"], 1)
            P.wait("vector", "rt_t", rtt)
            rv = P.inc("rt_v")

            @P.on("vector")
            def _(E, q=trq, t=tt):
                E.tensor_copy(out=rt_indT[:, t * 128:(t + 1) * 128],
                              in_=ps_bigS[q][0:8, 0:128]).then_inc(Se["rt_v"], 1)
            s_rb.commit("rt_v", rv)

        @P.on("vector")
        def _(E, first=(slot == 0)):
            if first:
                E.memset(gsml[0:8, 8:9], 0.0)
                E.drain()
            for c in range(4):
                E.tensor_copy(out=rt_sc[:, 0, :], in_=rt_indT[:, c * 352:(c + 1) * 352])
                E.drain()
                E.tensor_tensor_scan(out=rt_sc[:, 1, :], data0=rt_sc[:, 0, :],
                                     data1=rt_sc[:, 0, :], initial=gsml[0:8, 8:9],
                                     op0=ALU.add, op1=ALU.max)
                E.drain()
                E.tensor_copy(out=gsml[0:8, 8:9], in_=rt_sc[:, 1, 351:352])
                E.tensor_copy(out=rt_pref[:, c * 352:(c + 1) * 352], in_=rt_sc[:, 1, :])
                E.drain()
            E.tensor_scalar(out=rt_temp, in0=rt_pref, scalar1=2048.0, scalar2=None,
                            op0=ALU.is_gt)
            E.drain()
            E.tensor_scalar(out=rt_dst, in0=rt_pref, scalar1=ebs[:], scalar2=1048576.0,
                            op0=ALU.add, op1=ALU.add)
            E.drain()
            E.scalar_tensor_tensor(out=rt_dst, in0=rt_indT, scalar=-1048576.0, in1=rt_dst,
                                   op0=ALU.mult, op1=ALU.add)
            E.drain()
            E.scalar_tensor_tensor(out=rt_dst, in0=rt_temp, scalar=1048576.0, in1=rt_dst,
                                   op0=ALU.mult, op1=ALU.add).then_inc(Se["rt_v"], 1)
        dm_done = P.inc("rt_v")
        P.wait("tensor", "rt_v", dm_done)
        for tt in range(NQT):
            ci = slot * NQT + tt
            trq = s_rb.acquire("tensor")
            rtt = P.inc("rt_t")

            @P.on("tensor")
            def _(E, q=trq, t=tt):
                E.transpose(ps_bigS[q][:, 0:8], rt_dst[:, t * 128:(t + 1) * 128],
                            identf[0:8, 0:8]).then_inc(Se["rt_t"], 1)
            P.wait("vector", "rt_t", rtt)
            rv = P.inc("rt_v")

            @P.on("vector")
            def _(E, q=trq, c=ci):
                dT = gbuf[:, 2, :]
                E.tensor_copy(out=dT, in_=ps_bigS[q][:, 0:8])
                E.drain()
                E.tensor_reduce(gsml[:, 10:11], dT, axis=AX, op=ALU.min)
                E.drain()
                E.tensor_scalar(out=gbuf[:, 3, :], in0=dT, scalar1=gsml[:, 10:11],
                                scalar2=None, op0=ALU.is_equal)
                E.drain()
                E.scalar_tensor_tensor(out=gbuf[:, 4, :], in0=gbuf[:, 3, :], scalar=1e9,
                                       in1=dT, op0=ALU.mult, op1=ALU.add)
                E.drain()
                E.tensor_reduce(gsml[:, 11:12], gbuf[:, 4, :], axis=AX, op=ALU.min)
                E.drain()
                E.tensor_scalar(out=gbuf[:, 4, :], in0=dT, scalar1=gsml[:, 11:12],
                                scalar2=None, op0=ALU.is_equal)
                E.drain()
                E.tensor_mul(gbuf[:, 5, :], comb[:, c, :], gbuf[:, 3, :])
                E.drain()
                E.tensor_reduce(gsml[:, 12:13], gbuf[:, 5, :], axis=AX, op=ALU.add)
                E.drain()
                E.tensor_mul(gbuf[:, 5, :], comb[:, c, :], gbuf[:, 4, :])
                E.drain()
                E.tensor_reduce(gsml[:, 13:14], gbuf[:, 5, :], axis=AX, op=ALU.add)
                E.drain()
                E.tensor_scalar(out=gsml[:, 14:15], in0=gsml[:, 10:11], scalar1=16384.0,
                                scalar2=None, op0=ALU.is_lt)
                E.tensor_scalar(out=gsml[:, 15:16], in0=gsml[:, 11:12], scalar1=16384.0,
                                scalar2=None, op0=ALU.is_lt)
                E.drain()
                E.tensor_mul(cbuf[:, c, 0:1], gsml[:, 12:13], gsml[:, 14:15])
                E.tensor_mul(cbuf[:, c, 1:2], gsml[:, 13:14], gsml[:, 15:16])
                E.tensor_copy(out=didx[:, c, 0:1], in_=gsml[:, 10:11])
                E.tensor_copy(out=didx[:, c, 1:2], in_=gsml[:, 11:12]).then_inc(Se["rt_v"], 1)
            s_rb.commit("rt_v", rv)
            xp = xt_rb.acquire("gpsimd")
            P.wait("gpsimd", "rt_v", rv)
            P.wait("gpsimd", "xrow", (slot * NQT + tt + 1) * 16)
            xld = P.inc("xsc", 16)
            xs_ = P.inc("xsc", 32)

            @P.on("gpsimd")
            def _(E, p=xp, c=ci, sl=slot, t=tt, w=xld):
                E.dma_start(out=xrow_v[:, p, :],
                            in_=xnrow_s[sl * NQ + t * 128:sl * NQ + (t + 1) * 128, :]) \
                    .then_inc(Se["xsc"], 16)
                E.wait_ge(Se["xsc"], w)
                E.indirect_dma_start(
                    out=xg_s[:], out_offset=IndirectOffsetOnAxis(didx[:, c, 0:1], 0),
                    in_=xrow_v[:, p, :], in_offset=None).then_inc(Se["xsc"], 16)
                E.indirect_dma_start(
                    out=xg_s[:], out_offset=IndirectOffsetOnAxis(didx[:, c, 1:2], 0),
                    in_=xrow_v[:, p, :], in_offset=None).then_inc(Se["xsc"], 16)
            xt_rb.commit("xsc", xs_)
        if debug_taps and slot == 0:
            P.wait("sync", "rt_v", P.n("rt_v"))
            P.inc("tapd", 64)

            @P.on("sync")
            def _(E):
                E.dma_start(out=tap_d["t_rti"], in_=rt_indT).then_inc(Se["tapd"], 16)
                E.dma_start(out=tap_d["t_rtp"], in_=rt_dst).then_inc(Se["tapd"], 16)
                E.dma_start(out=tap_d["t_rtq"], in_=rt_pref).then_inc(Se["tapd"], 16)
                E.dma_start(out=tap_d["t_sc"], in_=rt_sc[:]).then_inc(Se["tapd"], 16)
        if debug_taps and slot == 1:
            P.wait("vector", "tapd", 32)

    def p3(slot):
        v65_ready = P.n("v65s")
        ones_ready = P.n("ones")
        kq_ready = P.n("kqe")
        P.wait("tensor", "v65s", v65_ready)
        P.wait("tensor", "ones", ones_ready)
        P.wait("tensor", "kqe", kq_ready)
        P.wait("tensor", "gate_v", P.n("gate_v"))
        routing(slot)
        for qci, (q0, qw, ntl) in enumerate(QCL):
            bops = {}
            for (kt, kind, cs, wd) in BAND_OPS[qci]:
                bops.setdefault(kt, []).append((kind, cs, wd))
            deferred = []
            for hp in range(heads // 2):
                h0, h1 = 2 * hp, 2 * hp + 1
                o0 = po_rb.acquire("tensor")
                o1 = po_rb.acquire("tensor")
                pend = []

                def emit_O_pair(fin):
                    k, t, ls, lv = pend.pop(0)
                    P.wait("tensor", ls, lv)
                    last = fin and not pend
                    if last:
                        som_ = P.inc("sOm")

                    @P.on("tensor")
                    def _(E, k=k, t=t, w=qw, A=h0, Bh=h1, a=o0, b=o1, last=last):
                        E.matmul(ps_o[a][0:65, 0:w], lhsT=V65[:, k, 65 * A:65 * A + 65],
                                 rhs=pt_buf[:, t, 0:w], start=(k == 0), stop=(k == KT_N - 1))
                        mm = E.matmul(ps_o[b][0:65, 0:w], lhsT=V65[:, k, 65 * Bh:65 * Bh + 65],
                                      rhs=pt_buf[:, t, 512:512 + w],
                                      start=(k == 0), stop=(k == KT_N - 1))
                        if last:
                            mm.then_inc(Se["sOm"], 1)
                    return P.n("sOm") if last else None

                for kt in range(KT_N):
                    sq_ = s_rb.acquire("tensor")
                    ss = P.inc("sS")

                    @P.on("tensor")
                    def _(E, k=kt, q=sq_, a=q0, w=qw, HP=hp):
                        E.matmul(ps_bigS[q][:, 0:w],
                                 lhsT=KT[0:64, HP, k * 128:(k + 1) * 128],
                                 rhs=QT[0:64, HP, a:a + w],
                                 start=True, stop=True, tile_position=(0, 0))
                        E.matmul(ps_bigS[q][:, 512:512 + w],
                                 lhsT=KT[64:128, HP, k * 128:(k + 1) * 128],
                                 rhs=QT[64:128, HP, a:a + w],
                                 start=True, stop=True,
                                 tile_position=(64, 0)).then_inc(Se["sS"], 1)
                    if kt == 1 and deferred:
                        for fn in deferred:
                            fn()
                        deferred = []
                    if pend:
                        emit_O_pair(False)
                    P.wait("scalar", "sS", ss)
                    sp = P.inc("sP")
                    ptp = (sp - 1) % 3

                    @P.on("scalar")
                    def _(E, q=sq_, t=ptp, fd=512 + qw):
                        E.activation(out=pt_buf[:, t, 0:fd], in_=ps_bigS[q][:, 0:fd],
                                     func=AF.Exp, scale=0.125).then_inc(Se["sP"], 1)
                    s_rb.commit("sP", sp)
                    last_sem, last_val = "sP", sp
                    if kt in bops:
                        P.wait("vector", "sP", sp)
                        for (kind, cs, wd) in bops[kt]:
                            if kind == "edge":
                                msrc = medge[:, slot, 0:wd]
                            elif kind == "wrap":
                                msrc = mwrap[:, slot, 0:wd]
                            else:
                                msrc = maskd[:, DELTAS.index(kind), 0:wd]
                            P.inc("sB")

                            @P.on("vector")
                            def _(E, t=ptp, c=cs, w=wd, m=msrc):
                                E.tensor_mul(pt_buf[:, t, c:c + w], pt_buf[:, t, c:c + w], m)
                            P.inc("sB")

                            @P.on("vector")
                            def _(E, t=ptp, c=512 + cs, w=wd, m=msrc):
                                E.tensor_mul(pt_buf[:, t, c:c + w], pt_buf[:, t, c:c + w],
                                             m).then_inc(Se["sB"], 2)
                        last_sem, last_val = "sB", P.n("sB")
                    pend.append((kt, ptp, last_sem, last_val))
                while len(pend) > 1:
                    emit_O_pair(False)
                som = emit_O_pair(True)
                P.wait("vector", "sOm", som)
                ot0 = ot_rb.acquire("vector")
                ot1 = ot_rb.acquire("vector")
                soe = P.inc("sOe")

                @P.on("vector")
                def _(E, a=o0, b=o1, t0=ot0, t1=ot1, w=qw):
                    E.tensor_copy(out=ot_stage[:, t0, 0:w], in_=ps_o[a][:, 0:w])
                    E.tensor_copy(out=ot_stage[:, t1, 0:w],
                                  in_=ps_o[b][:, 0:w]).then_inc(Se["sOe"], 1)
                po_rb.commit("sOe", soe)
                po_rb.commit("sOe", soe)

                def make_tail(h0=h0, h1=h1, ot0=ot0, ot1=ot1, soe=soe, n=ntl, t0q=q0 // 128):
                    def tail():
                        P.wait("tensor", "sOe", soe)
                        trq = s_rb.acquire("tensor")
                        stp = P.inc("sTp")

                        @P.on("tensor")
                        def _(E, q=trq):
                            for par, ot in ((0, ot0), (1, ot1)):
                                for c in range(n):
                                    E.transpose(ps_bigS[q][:, 256 * par + c * 64:256 * par + (c + 1) * 64],
                                                ot_stage[0:64, ot, c * 128:(c + 1) * 128],
                                                identf[0:64, 0:64])
                            for par, ot in ((0, ot0), (1, ot1)):
                                for c in range(n):
                                    mm = E.transpose(
                                        ps_bigS[q][:, 512 + 8 * par + c:512 + 8 * par + c + 1],
                                        ot_stage[64:65, ot, c * 128:(c + 1) * 128],
                                        identf[64:65, 64:65])
                            mm.then_inc(Se["sTp"], 1)
                        ot_rb.commit("sTp", stp)
                        ot_rb.commit("sTp", stp)
                        P.wait("vector", "sTp", stp)
                        son = P.inc("sOn")

                        @P.on("vector")
                        def _(E, q=trq):
                            E.tensor_copy(out=attn_out[:, 0:n, 64 * h0:64 * h0 + 64],
                                          in_=ps_bigS[q][:, 0:n * 64].rearrange("p (a b) -> p a b", b=64))
                            E.tensor_copy(out=attn_out[:, 0:n, 64 * h1:64 * h1 + 64],
                                          in_=ps_bigS[q][:, 256:256 + n * 64].rearrange("p (a b) -> p a b", b=64))
                            E.tensor_copy(out=Dm[:, t0q:t0q + n, h0:h0 + 1],
                                          in_=ps_bigS[q][:, 512:512 + n].rearrange("p (a b) -> p a b", b=1))
                            E.tensor_copy(out=Dm[:, t0q:t0q + n, h1:h1 + 1],
                                          in_=ps_bigS[q][:, 520:520 + n].rearrange("p (a b) -> p a b", b=1)) \
                                .then_inc(Se["sOn"], 1)
                        s_rb.commit("sOn", son)
                    return tail
                deferred.append(make_tail())
            for fn in deferred:
                fn()
            deferred = []
            if debug_taps is True and slot == 0:
                P.wait("sync", "sOn", P.n("sOn"))
                for tl in range(ntl):
                    td = P.inc("tapd", 16)

                    @P.on("sync")
                    def _(E, t=q0 // 128 + tl, c=tl):
                        E.dma_start(out=tap_d["t_attn"][t * 128:(t + 1) * 128, :],
                                    in_=attn_out[:, c, :]).then_inc(Se["tapd"], 16)
                P.wait("vector", "tapd", P.n("tapd"))
            for tl in range(ntl):
                tt_g = q0 // 128 + tl
                ao = P.inc("aout")

                @P.on("vector")
                def _(E, c=tl, t=tt_g, sl=slot):
                    E.tensor_scalar(out=dche[:], in0=Dm[:, t, :],
                                    scalar1=ccol[:, sl, t:t + 1], scalar2=None, op0=ALU.add)
                    E.drain()
                    E.reciprocal(drec[:], dche[:])
                    E.drain()
                    for H in range(15):
                        E.tensor_scalar(out=attn_out[:, c, 64 * H:64 * H + 64],
                                        in0=attn_out[:, c, 64 * H:64 * H + 64],
                                        scalar1=drec[:, H:H + 1], scalar2=None, op0=ALU.mult)
                    E.tensor_scalar(out=attn_out[:, c, 960:1024],
                                    in0=attn_out[:, c, 960:1024],
                                    scalar1=drec[:, 15:16], scalar2=None,
                                    op0=ALU.mult).then_inc(Se["aout"], 1)
                P.wait("sync", "aout", ao)
                so = P.inc("so", 16)

                @P.on("sync")
                def _(E, sl=slot, t=tt_g, c=tl):
                    E.dma_start(out=out_d[sl, t * 128:(t + 1) * 128, :],
                                in_=attn_out[:, c, :]).then_inc(Se["so"], 16)
            P.wait("vector", "so", P.n("so"))
        P.wait("gpsimd", "so", P.n("so"))
        P.inc("xacc", 16)

        P.inc("xacc", 16)

        @P.on("gpsimd")
        def _(E, sl=slot):
            E.dma_start(out=out_d[sl, 0:NQ:2, :], in_=xk_d[sl, 0:NQ:2, :],
                        accum_op=ALU.add).then_inc(Se["xacc"], 16)
            E.dma_start(out=out_d[sl, 1:NQ:2, :], in_=xk_d[sl, 1:NQ:2, :],
                        accum_op=ALU.add).then_inc(Se["xacc"], 16)

    def pC():
        P.wait("gpsimd", "sOn", P.n("sOn"))
        P.wait("gpsimd", "sP", P.n("sP"))
        P.wait("tensor", "sP", P.n("sP"))
        P.wait("gpsimd", "so", P.n("so"))
        P.wait("scalar", "so", P.n("so"))
        P.wait("sync", "sOn", P.n("sOn"))
        P.wait("sync", "xsc", P.n("xsc"))
        P.wait("tensor", "rt_v", P.n("rt_v"))

        w1bufs = [w1A, w1B]

        def load_w1(e):
            wp = w1_rb.acquire("gpsimd")
            cw = P.inc("cw", 16)

            @P.on("gpsimd")
            def _(E, a=w1_d[e], p=wp):
                E.dma_start(out=w1bufs[p], in_=a.rearrange("a p b -> p a b")).then_inc(Se["cw"], 16)
            return wp, cw

        def load_w2(e):
            w2_rb.acquire("gpsimd")
            cw = P.inc("cw", 16)

            @P.on("gpsimd")
            def _(E, a=w2_d[e]):
                E.dma_start(out=w2S, in_=a.rearrange("a p b -> p a b")).then_inc(Se["cw"], 16)
            return cw

        w1p, w1c = load_w1(0)
        w2c = load_w2(0)
        last_H = 0
        last_EO = 0
        for e in range(n_exp):
            if e + 1 < n_exp:
                nxt = load_w1(e + 1)
            for ch in range(4):
                base = 2048 * e + 512 * ch
                sqp = xq_rb.acquire("sync")
                cq = P.inc("cxq", 16)

                @P.on("sync")
                def _(E, b=base):
                    E.dma_start(out=cstage,
                                in_=xg_s[b:b + 512, :].rearrange("(a p) d -> p a d", p=128)) \
                        .then_inc(Se["cxq"], 16)
                P.wait("tensor", "cxq", cq)
                last_tr = 0
                for tl in range(4):
                    tpp = po_rb.acquire("tensor")
                    ctr = P.inc("cTr")
                    last_tr = ctr

                    @P.on("tensor")
                    def _(E, p=tl, t=tpp):
                        for dk in range(7):
                            E.transpose(ps_t[t][:, dk * 128:(dk + 1) * 128],
                                        cstage[:, p, dk * 128:(dk + 1) * 128], identb[:])
                        E.transpose(ps_t[t][:, 7 * 128:8 * 128],
                                    cstage[:, p, 7 * 128:8 * 128],
                                    identb[:]).then_inc(Se["cTr"], 1)
                    P.wait("vector", "cTr", ctr)
                    xgv = P.inc("cXg")

                    @P.on("vector")
                    def _(E, t=tpp, p=tl):
                        E.tensor_copy(out=XgT[:, :, p * 128:(p + 1) * 128],
                                      in_=ps_t[t].rearrange("p (a b) -> p a b", a=8)) \
                            .then_inc(Se["cXg"], 1)
                    po_rb.commit("cXg", xgv)
                xq_rb.commit("cTr", last_tr)
                hsp = hs_rb.acquire("scalar")
                P.wait("tensor", "cw", w1c)
                P.wait("tensor", "cXg", P.n("cXg"))
                for ht in range(16):
                    mp = mm_rb.acquire("tensor")
                    chh = P.inc("cH")
                    last_H = chh

                    @P.on("tensor")
                    def _(E, p=w1p, q=mp, H=ht):
                        for dk in range(7):
                            E.matmul(ps_mm[q][:, 0:512], lhsT=w1bufs[p][:, dk, H * 128:(H + 1) * 128],
                                     rhs=XgT[:, dk, :], start=(dk == 0), stop=False)
                        E.matmul(ps_mm[q][:, 0:512], lhsT=w1bufs[p][:, 7, H * 128:(H + 1) * 128],
                                 rhs=XgT[:, 7, :], start=False, stop=True).then_inc(Se["cH"], 1)
                    P.wait("scalar", "cH", chh)
                    cs = P.inc("cSi")

                    @P.on("scalar")
                    def _(E, q=mp, H=ht):
                        E.activation(out=Hsb[:, H, 0:512], in_=ps_mm[q][:, 0:512],
                                     func=AF.Silu).then_inc(Se["cSi"], 1)
                    mm_rb.commit("cSi", cs)
                P.wait("tensor", "cSi", P.n("cSi"))
                P.wait("tensor", "cw", w2c)
                for tl in range(4):
                    pb0 = po_rb.acquire("tensor")
                    pb1 = mm_rb.acquire("tensor")
                    banks = [ps_o[pb0][:], ps_mm[pb1]]
                    for dh in range(2):
                        ce = P.inc("cEOg")
                        last_EO = ce

                        @P.on("tensor")
                        def _(E, bk=banks[dh], c=tl, d=dh):
                            for ht in range(15):
                                E.matmul(bk[:, 0:512], lhsT=Hsb[:, ht, c * 128:(c + 1) * 128],
                                         rhs=w2S[:, ht, d * 512:(d + 1) * 512],
                                         start=(ht == 0), stop=False)
                            E.matmul(bk[:, 0:512], lhsT=Hsb[:, 15, c * 128:(c + 1) * 128],
                                     rhs=w2S[:, 15, d * 512:(d + 1) * 512],
                                     start=False, stop=True).then_inc(Se["cEOg"], 1)
                    P.wait("scalar", "cEOg", P.n("cEOg"))
                    ep = xt_rb.acquire("scalar")
                    cv = P.inc("cEv")

                    @P.on("scalar")
                    def _(E, b0=banks[0], b1=banks[1], p=ep):
                        E.activation(out=eo_stage[:, p, 0:512], in_=b0[:, 0:512], func=AF.Copy)
                        E.activation(out=eo_stage[:, p, 512:1024], in_=b1[:, 0:512],
                                     func=AF.Copy).then_inc(Se["cEv"], 1)
                    po_rb.commit("cEv", cv)
                    mm_rb.commit("cEv", cv)
                    P.wait("gpsimd", "cEv", cv)
                    co = P.inc("ceo", 16)

                    @P.on("gpsimd")
                    def _(E, p=ep, b=base, c=tl):
                        E.dma_start(out=eo_s[b + c * 128:b + (c + 1) * 128, :],
                                    in_=eo_stage[:, p, :]).then_inc(Se["ceo"], 16)
                    xt_rb.commit("ceo", co)
                hs_rb.commit("cEOg", last_EO)
            w1_rb.commit("cH", last_H)
            w2_rb.commit("cEOg", last_EO)
            if e + 1 < n_exp:
                w1p, w1c = nxt
                w2c = load_w2(e + 1)
        P.wait("gpsimd", "ceo", P.n("ceo"))
        gbufs = [gA0, gA1]
        tmps = [tmp0, tmp1]
        for tg in range(N_SLOT * NQT):
            pg = tg % 2
            if tg >= 2:
                P.wait("gpsimd", "cSum", tg - 1)
            cg = P.inc("cg", 32)

            @P.on("gpsimd")
            def _(E, g=gbufs[pg], t=tg):
                E.indirect_dma_start(
                    out=g[:, 0, :], out_offset=None, in_=eo_s[:],
                    in_offset=IndirectOffsetOnAxis(didx[:, t, 0:1], 0)).then_inc(Se["cg"], 16)
                E.indirect_dma_start(
                    out=g[:, 1, :], out_offset=None, in_=eo_s[:],
                    in_offset=IndirectOffsetOnAxis(didx[:, t, 1:2], 0)).then_inc(Se["cg"], 16)
            P.wait("vector", "cg", cg)
            if tg >= 2:
                P.wait("vector", "cAcc", (tg - 1) * 16)
            cm = P.inc("cSum")

            @P.on("vector")
            def _(E, g=gbufs[pg], m=tmps[pg], t=tg):
                E.tensor_scalar(out=m, in0=g[:, 0, :], scalar1=cbuf[:, t, 0:1],
                                scalar2=None, op0=ALU.mult)
                E.scalar_tensor_tensor(out=m, in0=g[:, 1, :], scalar=cbuf[:, t, 1:2],
                                       in1=m, op0=ALU.mult, op1=ALU.add).then_inc(Se["cSum"], 1)
            P.wait("gpsimd", "cSum", cm)
            ca = P.inc("cAcc", 16)

            @P.on("gpsimd")
            def _(E, m=tmps[pg], t=tg):
                E.dma_start(out=out_flat[t * 128:(t + 1) * 128, :], in_=m,
                            accum_op=ALU.add).then_inc(Se["cAcc"], 16)
        P.wait("gpsimd", "cAcc", P.n("cAcc"))
        P.wait("gpsimd", "xacc", P.n("xacc"))

    def p_taps_p2():
        P.wait("sync", "xnt", P.n("xnt"))
        for _ in range(3):
            P.inc("tapd", 16)

        @P.on("sync")
        def _(E):
            E.dma_start(out=tap_d["t_stats"], in_=stats[:]).then_inc(Se["tapd"], 16)
            E.dma_start(out=tap_d["t_xn"], in_=xn_tile[:]).then_inc(Se["tapd"], 16)
            E.dma_start(out=tap_d["t_xt"], in_=x_tile[:]).then_inc(Se["tapd"], 16)
        P.wait("sync", "kqe", P.n("kqe"))
        P.wait("sync", "v65s", P.n("v65s"))
        P.wait("sync", "ones", P.n("ones"))
        P.wait("sync", "combr", P.n("combr"))
        for _ in range(5):
            P.inc("tapd", 16)

        @P.on("sync")
        def _(E):
            E.dma_start(out=tap_d["t_xnt"].rearrange("a p b -> p a b"),
                        in_=XnT[:]).then_inc(Se["tapd"], 16)
            E.dma_start(out=tap_d["t_kt"].rearrange("a p b -> p a b"),
                        in_=KT[:]).then_inc(Se["tapd"], 16)
            E.dma_start(out=tap_d["t_qt"].rearrange("a p b -> p a b"),
                        in_=QT[:]).then_inc(Se["tapd"], 16)
            E.dma_start(out=tap_d["t_v65"].rearrange("a p b -> p a b"),
                        in_=V65[:]).then_inc(Se["tapd"], 16)
            E.dma_start(out=tap_d["t_comb"], in_=comb[:, 0:NQT, :]).then_inc(Se["tapd"], 16)

    p_init()
    for slot in range(n_slot):
        p1(slot)
        p2(slot)
        if debug_taps is True and slot == 0:
            p_taps_p2()
        if do_attn:
            p3(slot)
    if do_ffn:
        pC()
    else:
        P.wait("sync", "so", P.n("so"))
    if debug_taps:
        P.wait("sync", "rt_v", P.n("rt_v"))
        P.wait("sync", "ceo", P.n("ceo"))
        P.wait("sync", "cAcc", P.n("cAcc"))
        for _ in range(5):
            P.inc("tapd", 16)

        @P.on("sync")
        def _(E):
            E.dma_start(out=tap_d["t_didx"], in_=didx[:]).then_inc(Se["tapd"], 16)
            E.dma_start(out=tap_d["t_cbuf"], in_=cbuf[:]).then_inc(Se["tapd"], 16)
            E.dma_start(out=tap_d["t_ind"], in_=ind_all[:]).then_inc(Se["tapd"], 16)
            E.dma_start(out=tap_d["t_xg"], in_=xg_s[:]).then_inc(Se["tapd"], 16)
            E.dma_start(out=tap_d["t_eo"], in_=eo_s[:]).then_inc(Se["tapd"], 16)
        P.wait("sync", "tapd", P.n("tapd"))

    blk = es.enter_context(nc.Block())
    for eng in Plan.ENGINES:
        def body(E, ops=tuple(P.ops[eng])):
            for fn in ops:
                fn(E)
        getattr(blk, eng)(body)
    es.close()
    return nc


def stage_core_inputs(core, x, W_q, W_k, W_v, gate_w, balance_bias, w1, w2, shared):
    xk = np.stack([stage_xk(x, core, j) for j in range(N_SLOT)])
    cc = np.stack([stage_ccol(core, j) for j in range(N_SLOT)])
    ccol = np.ascontiguousarray(cc.reshape(N_SLOT, NQT, 128).transpose(2, 0, 1)).astype(np.float32)
    medge = np.stack([stage_edge_mask(core, j) for j in range(N_SLOT)]).astype(np.float32)
    mwrap = np.stack([stage_wrap_mask(core, j) for j in range(N_SLOT)]).astype(np.float32)
    m = dict(shared)
    m.update(xk=np.ascontiguousarray(xk), ccol=ccol,
             medge=np.ascontiguousarray(medge), mwrap=np.ascontiguousarray(mwrap))
    return m


def make_shared(W_q, W_k, W_v, gate_w, balance_bias, w1, w2):
    wq = np.ascontiguousarray(W_q.reshape(8, 128, 8, 128).transpose(2, 0, 1, 3))
    wk = np.ascontiguousarray(W_k.reshape(8, 128, 8, 128).transpose(2, 0, 1, 3))
    wv65 = np.zeros((1024, 1040), dtype=np.float32)
    for h in range(HEADS):
        wv65[:, 65 * h:65 * h + 64] = W_v[:, 64 * h:64 * h + 64]
    wv65 = np.ascontiguousarray(wv65.reshape(8, 128, 1040))
    gwt = np.ascontiguousarray(gate_w.T.reshape(8, 128, 8))
    bb = np.ascontiguousarray(np.tile(balance_bias.reshape(1, 8), (128, 1))).astype(np.float32)
    w1t = np.ascontiguousarray(w1.transpose(0, 2, 1).reshape(8, 8, 128, DH))
    w2t = np.ascontiguousarray(w2.transpose(0, 2, 1).reshape(8, 16, 128, D))
    return dict(wq=wq, wk=wk, wv65=wv65, gwt=gwt, bbias=bb, w1t=w1t, w2t=w2t)


def kernel(L0_attn_x, L0_attn_W_q, L0_attn_W_k, L0_attn_W_v,
           L0_ffn_gate_w, L0_ffn_balance_bias, L0_ffn_w1, L0_ffn_w2):
    x = np.asarray(L0_attn_x, dtype=np.float32)
    args = [np.asarray(a, dtype=np.float32) for a in
            (L0_attn_W_q, L0_attn_W_k, L0_attn_W_v, L0_ffn_gate_w,
             L0_ffn_balance_bias, L0_ffn_w1, L0_ffn_w2)]
    shared = make_shared(*args)
    nc = build_program()
    in_maps = [stage_core_inputs(c, x, *args, shared) for c in range(8)]
    res = run_bass_kernel_spmd(nc, in_maps, core_ids=list(range(8)))
    out = np.zeros((B, L, D), dtype=np.float32)
    for c in range(8):
        oc = res.results[c]["out"]
        for j in range(N_SLOT):
            b, s, half, off, hlen = slot_meta(c, j)
            i = np.arange(hlen)
            pos = 3 * (off + i) + s
            keep = pos <= L - 1
            out[b, pos[keep], :] = oc[j, i[keep], :]
    return out
